# version 04 (final)
# speedup vs baseline: 1.1565x; 1.0574x over previous
import contextlib
import numpy as np
import concourse.bass as bass
import concourse.mybir as mybir
from concourse.bass_utils import run_bass_kernel_spmd

F32 = mybir.dt.float32
BF16 = mybir.dt.bfloat16
AF = mybir.ActivationFunctionType
ALU = mybir.AluOpType
AX = mybir.AxisListType

D = 2048
HB_W1 = 2
SEQ = 4096
NB = 4
TOK = 2048
QT = 1024
KC = 16
NCORE = 8
LN_EPS = 1e-5
F32_EPS = float(np.finfo(np.float32).eps)


def shard_x(x):
    out = []
    for c in range(NCORE):
        b, p = c // 2, c % 2
        xt = np.asarray(x[b, p * TOK:(p + 1) * TOK])
        out.append(np.ascontiguousarray(xt.T.astype(np.float32, copy=False)))
    return out


def unshard_x(parts):
    x = np.empty((NB, SEQ, D), np.float32)
    for c in range(NCORE):
        b, p = c // 2, c % 2
        x[b, p * TOK:(p + 1) * TOK] = np.ascontiguousarray(np.asarray(parts[c]).T)
    return x



class Sem:
    __slots__ = ("h", "v")

    def __init__(self, ctx, nc, name):
        self.h = ctx.enter_context(nc.semaphore(name))
        self.v = 0

    def inc(self, n=1):
        self.v += n
        return self.h


class _ChainIns:

    __slots__ = ("ew",)

    def __init__(self, ew):
        self.ew = ew

    def then_inc(self, sem_h, n):
        e = self.ew.e
        e.wait_ge(self.ew.sem.h, self.ew.sem.v)
        e.sem_inc(sem_h, n)
        return self


class EngWrap:

    COMPUTE = {
        "tensor_tensor", "tensor_copy", "tensor_add", "tensor_sub",
        "tensor_mul", "tensor_max", "tensor_scalar", "tensor_single_scalar",
        "tensor_scalar_add", "tensor_scalar_sub", "tensor_scalar_mul",
        "tensor_scalar_max", "tensor_scalar_min", "tensor_relu",
        "tensor_reduce", "reduce_max", "reduce_sum", "scalar_tensor_tensor",
        "tensor_tensor_scan", "memset", "memzero", "reciprocal", "select",
        "activation", "copy", "mul", "add", "square", "sqrt", "sign",
        "iota", "affine_select", "transpose", "bn_stats", "bn_aggr",
    }
    DMA = {"dma_start", "dma_start_transpose", "indirect_dma_start",
           "collective_compute"}

    def __init__(self, e, sem):
        self.e = e
        self.sem = sem

    def __getattr__(self, nm):
        real = getattr(self.e, nm)
        if nm in self.COMPUTE:
            def f(*a, **kw):
                self.e.wait_ge(self.sem.h, self.sem.v)
                ins = real(*a, **kw)
                ins.then_inc(self.sem.h, 1)
                self.sem.v += 1
                return _ChainIns(self)
            return f
        if nm in self.DMA:
            def f(*a, **kw):
                self.e.wait_ge(self.sem.h, self.sem.v)
                return real(*a, **kw)
            return f
        return real


class Ctx:

    SEMS = ("lx", "lw", "lw2", "ls1", "lxr", "xh", "mm", "mm2", "mm3", "ev",
            "ev2", "g1", "g2", "g3", "g4", "st", "st2", "cc", "aux",
            "lt1", "lt2", "ga3", "ga4", "ge4", "tp", "te", "li", "te2")

    def __init__(self, nc, ctx):
        self.nc = nc
        self.ctx = ctx
        self.s = {n: Sem(ctx, nc, "sem_" + n) for n in self.SEMS}
        self.chain = {n: Sem(ctx, nc, "chain_" + n)
                      for n in ("vector", "scalar", "gpsimd")}
        self.XH = ctx.enter_context(nc.sbuf_tensor("XH", [128, KC, TOK], BF16))
        self.ones_bf = ctx.enter_context(nc.sbuf_tensor("ones_bf", [128, 128], BF16))
        self.mu = ctx.enter_context(nc.sbuf_tensor("mu", [128, TOK], F32))
        self.rstd = ctx.enter_context(nc.sbuf_tensor("rstd", [128, TOK], F32))
        self.sq = ctx.enter_context(nc.sbuf_tensor("sq", [128, 2, 512], BF16))
        self.tmp1 = ctx.enter_context(nc.sbuf_tensor("tmp1", [128, 512], F32))
        self.pb = [ctx.enter_context(nc.psum_tensor(f"pb{i}", [128, 512], F32))
                   for i in range(8)]


def emit_consts(block, C):
    @block.vector
    def _(e):
        e = EngWrap(e, C.chain["vector"])
        e.memset(C.ones_bf[:], 1.0).then_inc(C.s["aux"].inc(), 1)


def emit_ln(block, C, x_dram):
    ntt = TOK // 512
    lx, xh, mm, mm3, ev, ev2, g2 = (C.s[n] for n in
                                    ("lx", "xh", "mm", "mm3", "ev", "ev2", "g2"))
    b_lx, b_xh, b_mm, b_mm3, b_ev, b_ev2, b_g2 = (
        lx.v, xh.v, mm.v, mm3.v, ev.v, ev2.v, g2.v)

    def lx_tile_done(tt):
        return b_lx + 16 * KC * min(tt + 2, ntt)

    @block.gpsimd
    def _(e):
        e = EngWrap(e, C.chain["gpsimd"])
        for tt in range(ntt):
            if tt >= 2:
                e.wait_ge(mm3.h, b_mm3 + (tt - 1) * KC)
            for k in range(KC):
                e.dma_start(
                    out=C.XH[:, k, tt * 512:(tt + 1) * 512],
                    in_=x_dram[k * 128:(k + 1) * 128, tt * 512:(tt + 1) * 512],
                ).then_inc(lx.h, 16)
    lx.v += 16 * ntt * KC

    @block.scalar
    def _(e):
        e = EngWrap(e, C.chain["scalar"])
        for tt in range(ntt):
            e.wait_ge(lx.h, lx_tile_done(tt))
            for k in range(KC):
                i = tt * KC + k
                if i >= 2:
                    e.wait_ge(mm3.h, b_mm3 + i - 1)
                e.activation(C.sq[:, i % 2, :], C.XH[:, k, tt * 512:(tt + 1) * 512],
                             AF.Square).then_inc(xh.h, 1)
    xh.v += ntt * KC

    @block.tensor
    def _(e):
        e.wait_ge(C.s["aux"].h, 1)
        for tt in range(ntt):
            pa = C.pb[4 + (tt % 2) * 2]
            pq = C.pb[5 + (tt % 2) * 2]
            if tt >= 2:
                e.wait_ge(ev.h, b_ev + tt - 1)
            e.wait_ge(lx.h, lx_tile_done(tt))
            for k in range(KC):
                i = tt * KC + k
                e.matmul(pa[:], C.ones_bf[:], C.XH[:, k, tt * 512:(tt + 1) * 512],
                         start=(k == 0), stop=(k == KC - 1))
                e.wait_ge(xh.h, b_xh + i + 1)
                ins = e.matmul(pq[:], C.ones_bf[:], C.sq[:, i % 2, :],
                               start=(k == 0), stop=(k == KC - 1))
                ins.then_inc(mm3.h, 1)
    mm3.v += ntt * KC

    @block.vector
    def _(e):
        e = EngWrap(e, C.chain["vector"])
        for tt in range(ntt):
            pa = C.pb[4 + (tt % 2) * 2]
            pq = C.pb[5 + (tt % 2) * 2]
            sl = slice(tt * 512, (tt + 1) * 512)
            e.wait_ge(mm3.h, b_mm3 + (tt + 1) * KC)
            e.tensor_scalar_mul(C.mu[:, sl], pa[:], 1.0 / D)
            e.tensor_scalar_mul(C.rstd[:, sl], pq[:], 1.0 / D)
            e.tensor_tensor(C.tmp1[:], C.mu[:, sl], C.mu[:, sl], op=ALU.mult)
            e.tensor_tensor(C.rstd[:, sl], C.rstd[:, sl], C.tmp1[:], op=ALU.subtract)
            e.tensor_scalar_add(C.rstd[:, sl], C.rstd[:, sl], LN_EPS)
            e.reciprocal(C.rstd[:, sl], C.rstd[:, sl]).then_inc(ev.h, 1)
    ev.v += ntt

    @block.scalar
    def _(e):
        e = EngWrap(e, C.chain["scalar"])
        for tt in range(ntt):
            sl = slice(tt * 512, (tt + 1) * 512)
            e.wait_ge(ev.h, b_ev + tt + 1)
            e.activation(C.rstd[:, sl], C.rstd[:, sl], AF.Sqrt).then_inc(ev2.h, 1)
    ev2.v += ntt

    @block.vector
    def _(e):
        e = EngWrap(e, C.chain["vector"])
        for tt in range(ntt):
            sl = slice(tt * 512, (tt + 1) * 512)
            e.wait_ge(ev2.h, b_ev2 + tt + 1)
            for k in range(KC):
                e.tensor_tensor(C.XH[:, k, sl], C.XH[:, k, sl], C.mu[:, sl],
                                op=ALU.subtract)
                ins = e.tensor_tensor(C.XH[:, k, sl], C.XH[:, k, sl],
                                      C.rstd[:, sl], op=ALU.mult)
            ins.then_inc(g2.h, 1)
    g2.v += ntt
    return b_g2



def emit_ternarize(block, C, w_dram, wt_dram, rows, cols, tern, colsplit=1,
                   ident=None, ident_ready=None):
    cw = cols // colsplit
    ng = cw // 64
    nt = rows // 128
    BL = cw // 128
    lw, g1, g3, xh = C.s["lw"], C.s["g1"], C.s["g3"], C.s["xh"]
    tp, te, te2 = C.s["tp"], C.s["te"], C.s["te2"]
    b_lw, b_g1, b_g3, b_xh = lw.v, g1.v, g3.v, xh.v
    b_tp, b_te, b_te2 = tp.v, te.v, te2.v
    RB, SB = 32, 32
    NTI = nt * colsplit

    wb = [tern["wb"][:, b, 0:cw] for b in range(2)]
    tq = [tern["tq"][:, b, 0:cw] for b in range(2)]
    ab = tern["ab"][:, 0:cw]
    sc = tern["sc"][:, 0:ng]
    scb = tern["scb"][:, 0:ng]
    rec = tern["rec"][:, 0:ng]
    tb = tern["tb"]
    pbT = [C.pb[2].ap().bitcast(BF16)[:, 0:128],
           C.pb[3].ap().bitcast(BF16)[:, 0:128]]

    def tile_rc(i):
        return i // colsplit, i % colsplit

    @block.gpsimd
    def _(e):
        e = EngWrap(e, C.chain["gpsimd"])
        for i in range(NTI + 2):
            if i < NTI:
                r, cs = tile_rc(i)
                if i >= 2:
                    e.wait_ge(g1.h, b_g1 + i - 1)
                elif b_g1:
                    e.wait_ge(g1.h, b_g1)
                e.dma_start(out=wb[i % 2],
                            in_=w_dram[r * 128:(r + 1) * 128,
                                       cs * cw:(cs + 1) * cw]).then_inc(lw.h, 16)
            if i >= 2:
                j = i - 2
                r, cs = tile_rc(j)
                for c in range(BL):
                    q = j * BL + c
                    e.wait_ge(te.h, b_te + q + 1)
                    e.dma_start(
                        out=wt_dram[cs * cw + c * 128:cs * cw + (c + 1) * 128,
                                    r * 128:(r + 1) * 128],
                        in_=tb[:, q % RB, :]).then_inc(g3.h, 16)
    lw.v += 16 * NTI
    g3.v += 16 * NTI * BL

    @block.scalar
    def _(e):
        e = EngWrap(e, C.chain["scalar"])

        def evac(t):
            for c in range(BL):
                q = t * BL + c
                e.wait_ge(tp.h, b_tp + q + 1)
                if q >= RB:
                    e.wait_ge(te2.h, b_te2 + (q - RB) // SB + 1)
                elif b_te2:
                    e.wait_ge(te2.h, b_te2)
                e.copy(tb[:, q % RB, :], pbT[q % 2]).then_inc(te.h, 1)

        for i in range(NTI):
            e.wait_ge(lw.h, b_lw + 16 * min(i + 2, NTI))
            if i >= 1:
                e.wait_ge(g1.h, b_g1 + i)
            elif b_g1:
                e.wait_ge(g1.h, b_g1)
            e.activation(ab, wb[i % 2], AF.Abs).then_inc(xh.h, 1)
            if i >= 1:
                evac(i - 1)
        evac(NTI - 1)
    xh.v += NTI
    te.v += NTI * BL

    NB = (NTI * BL) // SB
    @block.sync
    def _(e):
        for k in range(NB):
            e.wait_ge(g3.h, b_g3 + 16 * SB * (k + 1))
            e.sem_inc(te2.h, 1)
    te2.v += NB

    @block.tensor
    def _(e):
        if ident_ready is not None:
            e.wait_ge(*ident_ready)
        for j in range(NTI):
            e.wait_ge(g1.h, b_g1 + j + 1)
            for c in range(BL):
                q = j * BL + c
                if q >= 2:
                    e.wait_ge(te.h, b_te + q - 1)
                ins = e.transpose(pbT[q % 2], tq[j % 2][:, c * 128:(c + 1) * 128],
                                  ident)
                ins.then_inc(tp.h, 1)
    tp.v += NTI * BL

    @block.vector
    def _(e):
        e = EngWrap(e, C.chain["vector"])
        a3 = ab.rearrange("p (g s) -> p g s", s=64)
        aB = ab.rearrange("p (g s) -> p s g", s=64)
        rB = rec.rearrange("p (o g) -> p o g", o=1).broadcast_to((128, 64, ng))
        sB = scb.rearrange("p (o g) -> p o g", o=1).broadcast_to((128, 64, ng))
        for i in range(NTI):
            e.wait_ge(xh.h, b_xh + i + 1)
            if i >= 2:
                e.wait_ge(tp.h, b_tp + (i - 1) * BL)
            elif b_tp:
                e.wait_ge(tp.h, b_tp)
            wB = wb[i % 2].rearrange("p (g s) -> p s g", s=64)
            gt3 = tq[i % 2].rearrange("p (g s) -> p g s", s=64)
            gtB = tq[i % 2].rearrange("p (g s) -> p s g", s=64)
            e.tensor_reduce(sc, a3, axis=AX.X, op=ALU.add)
            e.tensor_scalar_mul(sc, sc, 1.0 / 64.0)
            e.tensor_scalar_max(sc, sc, 1e-8)
            e.tensor_copy(scb, sc)
            e.reciprocal(rec, scb)
            e.tensor_tensor(aB, wB, rB, op=ALU.mult)
            e.tensor_single_scalar(gt3, a3, 0.5, op=ALU.is_gt)
            e.tensor_single_scalar(a3, a3, -0.5, op=ALU.is_lt)
            e.tensor_tensor(gt3, gt3, a3, op=ALU.subtract)
            ins = e.tensor_tensor(gtB, gtB, sB, op=ALU.mult)
            ins.then_inc(g1.h, 1)
    g1.v += NTI
    return g3.v



def emit_l1_ffn_body(block, C, x_dram, xout_dram, w1t_dram, w2t_dram, f,
                     xh_marker, tern_done):
    HB = HB_W1
    lw, lw2, mm, mm2, mm3, ev, ev2, g4, st, lx, xh = (
        C.s[n] for n in ("lt1", "lt2", "mm", "mm2", "mm3", "ev", "ev2",
                         "ge4", "st", "lxr", "xh"))

    for grp in range(4):
        t0 = grp * 512
        b_lw, b_mm, b_mm2, b_ev, b_xh = lw.v, mm.v, mm2.v, ev.v, xh.v
        b_ev2, b_g4, b_st, b_lx, b_lw2, b_mm3 = (ev2.v, g4.v, st.v, lx.v,
                                                 lw2.v, mm3.v)

        @block.sync
        def _(e):
            if grp == 0:
                e.wait_ge(C.s["g3"].h, tern_done)
            else:
                e.wait_ge(mm.h, b_mm)
            for s in range(64 // HB):
                if s >= 2:
                    e.wait_ge(mm.h, b_mm + (s - 1) * HB)
                for k in range(KC):
                    e.dma_start(
                        out=f["w1s"][:, s % 2, k, :],
                        in_=w1t_dram[k * 128:(k + 1) * 128,
                                     s * HB * 128:(s + 1) * HB * 128],
                    ).then_inc(lw.h, 16)
        lw.v += 16 * (64 // HB) * KC

        @block.tensor
        def _(e):
            if grp == 0:
                e.wait_ge(C.s["g2"].h, xh_marker + 4)
            else:
                e.wait_ge(g4.h, b_g4)
            for hc in range(64):
                s, hj = hc // HB, hc % HB
                ph = C.pb[hc % 2]
                if hj == 0:
                    e.wait_ge(lw.h, b_lw + 16 * KC * min(s + 2, 64 // HB))
                if hc >= 2:
                    e.wait_ge(ev.h, b_ev + hc - 1)
                for k in range(KC):
                    ins = e.matmul(ph[:],
                                   f["w1s"][:, s % 2, k, hj * 128:(hj + 1) * 128],
                                   C.XH[:, k, t0:t0 + 512],
                                   start=(k == 0), stop=(k == KC - 1),
                                   skip_group_check=True)
                ins.then_inc(mm.h, 1)
                if hc >= 1:
                    e.wait_ge(xh.h, b_xh + hc)
                    e.matmul(C.pb[2][:], C.ones_bf[:], f["h2"][:, (hc - 1) % 2, :],
                             start=(hc == 1), stop=False, skip_group_check=True)
            e.wait_ge(xh.h, b_xh + 64)
            ins = e.matmul(C.pb[2][:], C.ones_bf[:], f["h2"][:, 1, :],
                           start=False, stop=True, skip_group_check=True)
            ins.then_inc(mm2.h, 1)
        mm.v += 64
        mm2.v += 1

        @block.scalar
        def _(e):
            e = EngWrap(e, C.chain["scalar"])
            for hc in range(64):
                e.wait_ge(mm.h, b_mm + hc + 1)
                if hc >= 2:
                    e.wait_ge(xh.h, b_xh + hc - 1)
                e.activation(f["rl"][:, hc % 2, :], C.pb[hc % 2][:],
                             AF.Relu).then_inc(ev.h, 1)
        ev.v += 64

        @block.vector
        def _(e):
            e = EngWrap(e, C.chain["vector"])
            if grp == 0:
                e.wait_ge(C.s["g3"].h, tern_done)
            for hc in range(64):
                e.wait_ge(ev.h, b_ev + hc + 1)
                e.tensor_tensor(f["H"][:, hc, :], f["rl"][:, hc % 2, :],
                                f["rl"][:, hc % 2, :], op=ALU.mult)
                ins = e.tensor_tensor(f["h2"][:, hc % 2, :], f["H"][:, hc, :],
                                      f["H"][:, hc, :], op=ALU.mult)
                ins.then_inc(xh.h, 1)
        xh.v += 64

        @block.vector
        def _(e):
            e = EngWrap(e, C.chain["vector"])
            e.wait_ge(mm2.h, b_mm2 + 1)
            e.tensor_scalar_mul(f["rt"][:], C.pb[2][:], 1.0 / 8192.0)
            e.tensor_scalar_add(f["rt"][:], f["rt"][:], F32_EPS)
            e.reciprocal(f["rt"][:], f["rt"][:]).then_inc(ev.h, 1)
        ev.v += 1

        @block.scalar
        def _(e):
            e = EngWrap(e, C.chain["scalar"])
            e.wait_ge(ev.h, b_ev + 65)
            e.activation(f["rt"][:], f["rt"][:], AF.Sqrt).then_inc(ev2.h, 1)
        ev2.v += 1

        if C.dbg:
            @block.gpsimd
            def _(e):
                e = EngWrap(e, C.chain["gpsimd"])
                e.wait_ge(ev2.h, ev2.v)
                e.dma_start(out=C.dbg["o_rt"][grp], in_=f["rt"][:]
                            ).then_inc(C.s["g3"].inc(16), 16)

        @block.sync
        def _(e):
            if grp > 0:
                e.wait_ge(g4.h, b_g4)
            for dcb in range(2):
                for dc in range(8):
                    i = dcb * 8 + dc
                    if i >= 8:
                        e.wait_ge(g4.h, b_g4 + i - 7)
                    e.dma_start(out=f["xr"][:, i % 8, :],
                                in_=x_dram[i * 128:(i + 1) * 128, t0:t0 + 512],
                                ).then_inc(lx.h, 16)
                for hc in range(64):
                    j = dcb * 64 + hc
                    if j >= 2:
                        e.wait_ge(mm3.h, b_mm3 + j - 1)
                    e.dma_start(
                        out=f["w2s"][:, j % 2, :],
                        in_=w2t_dram[hc * 128:(hc + 1) * 128,
                                     dcb * 1024:(dcb + 1) * 1024],
                    ).then_inc(lw2.h, 16)
                for dc in range(8):
                    i = dcb * 8 + dc
                    e.wait_ge(g4.h, b_g4 + i + 1)
                    e.dma_start(out=xout_dram[i * 128:(i + 1) * 128, t0:t0 + 512],
                                in_=f["yt"][:, i % 8, :]).then_inc(st.h, 16)
        lx.v += 16 * KC
        lw2.v += 16 * 128
        st.v += 16 * KC

        @block.tensor
        def _(e):
            e.wait_ge(ev.h, b_ev + 65)
            for dcb in range(2):
                if dcb == 1:
                    e.wait_ge(g4.h, b_g4 + 8)
                for hc in range(64):
                    j = dcb * 64 + hc
                    e.wait_ge(lw2.h, b_lw2 + 16 * min(j + 2, (dcb + 1) * 64))
                    for dc in range(8):
                        ins = e.matmul(C.pb[dc][:],
                                       f["w2s"][:, j % 2, dc * 128:(dc + 1) * 128],
                                       f["H"][:, hc, :],
                                       start=(hc == 0), stop=(hc == 63),
                                       skip_group_check=True)
                    ins.then_inc(mm3.h, 1)
        mm3.v += 128

        @block.vector
        def _(e):
            e = EngWrap(e, C.chain["vector"])
            e.wait_ge(ev2.h, b_ev2 + 1)
            if grp > 0:
                e.wait_ge(st.h, b_st)
            for dcb in range(2):
                e.wait_ge(mm3.h, b_mm3 + (dcb + 1) * 64)
                e.wait_ge(lx.h, b_lx + 16 * 8 * (dcb + 1))
                for dc in range(8):
                    i = dcb * 8 + dc
                    if i >= 8:
                        e.wait_ge(st.h, b_st + 16 * i)
                    e.tensor_tensor(f["yt"][:, i % 8, :], C.pb[dc][:], f["rt"][:],
                                    op=ALU.mult)
                    ins = e.tensor_tensor(f["yt"][:, i % 8, :], f["yt"][:, i % 8, :],
                                          f["xr"][:, i % 8, :], op=ALU.add)
                    ins.then_inc(g4.h, 1)
        g4.v += KC


def emit_final_wait(block, C):
    @block.sync
    def _(e):
        for n in ("st", "st2", "g3", "g4", "lxr", "ga3", "ga4"):
            if C.s[n].v:
                e.wait_ge(C.s[n].h, C.s[n].v)



def build_l1_ffn_program(detect_races=True, dbg=False):
    nc = bass.Bass("TRN2", target_bir_lowering=False, debug=False,
                   detect_race_conditions=detect_races)
    x_in = nc.dram_tensor("x_in", [D, TOK], F32, kind="ExternalInput").ap()
    w_fc = nc.dram_tensor("w_fc", [8192, D], F32, kind="ExternalInput").ap()
    w_proj = nc.dram_tensor("w_proj", [D, 8192], F32, kind="ExternalInput").ap()
    x_out = nc.dram_tensor("x_out", [D, TOK], F32, kind="ExternalOutput").ap()
    w1t = nc.dram_tensor("w1t", [D, 8192], BF16, kind="Internal").ap()
    w2t = nc.dram_tensor("w2t", [8192, D], BF16, kind="Internal").ap()
    c_ident = nc.dram_tensor("f_ident", [128, 128], BF16, kind="ExternalInput").ap()

    with contextlib.ExitStack() as ctx:
        C = Ctx(nc, ctx)
        arena = ctx.enter_context(nc.sbuf_tensor("arena", [128, 32768], BF16))
        ar = arena.ap()
        tern = {
            "wb": ar[:, 0:8192].rearrange("p (b c) -> p b c", b=2),
            "ab": ar[:, 8192:12288],
            "tq": ar[:, 12288:20480].rearrange("p (b c) -> p b c", b=2),
        }
        sc = ctx.enter_context(nc.sbuf_tensor("sc", [128, 64], F32))
        scb = ctx.enter_context(nc.sbuf_tensor("scb", [128, 64], BF16))
        rec = ctx.enter_context(nc.sbuf_tensor("rec", [128, 64], F32))
        tbt = ar[:, 24576:28672].rearrange("p (r m) -> p r m", m=128)
        idt = ar[:, 28672:28800]
        tern.update(sc=sc.ap(), scb=scb.ap(), rec=rec.ap(), tb=tbt)

        f = {
            "H": ar[:, 0:32768].rearrange("p (h t) -> p h t", h=64),
            "w1s": ctx.enter_context(
                nc.sbuf_tensor("w1s", [128, 2, KC, HB_W1 * 128], BF16)),
            "w2s": ctx.enter_context(nc.sbuf_tensor("w2s", [128, 2, 1024], BF16)),
            "rl": ctx.enter_context(nc.sbuf_tensor("rl", [128, 2, 512], BF16)),
            "h2": ctx.enter_context(nc.sbuf_tensor("h2", [128, 2, 512], BF16)),
            "rt": ctx.enter_context(nc.sbuf_tensor("rt", [128, 512], F32)),
            "yt": ctx.enter_context(nc.sbuf_tensor("yt", [128, 8, 512], F32)),
            "xr": ctx.enter_context(nc.sbuf_tensor("xr", [128, 8, 512], F32)),
        }

        if dbg:
            o_w1t = nc.dram_tensor("o_w1t", [8192, D], BF16, kind="ExternalOutput").ap()
            o_rt = nc.dram_tensor("o_rt", [4, 128, 512], F32, kind="ExternalOutput").ap()
            o_H = nc.dram_tensor("o_H", [128, 64, 512], BF16, kind="ExternalOutput").ap()
            C.dbg = {"o_w1t": o_w1t, "o_rt": o_rt, "o_H": o_H}
        else:
            C.dbg = None

        block = ctx.enter_context(nc.Block())
        emit_consts(block, C)

        @block.gpsimd
        def _(e):
            e.dma_start(out=idt, in_=c_ident).then_inc(C.s["li"].h, 16)
        C.s["li"].v += 16

        xh_marker = emit_ln(block, C, x_in)
        irdy = (C.s["li"].h, 16)
        emit_ternarize(block, C, w_fc, w1t, 8192, D, tern, colsplit=1,
                       ident=idt, ident_ready=irdy)
        emit_ternarize(block, C, w_proj, w2t, D, 8192, tern, colsplit=2,
                       ident=idt, ident_ready=irdy)
        tern_done = C.s["g3"].v
        emit_l1_ffn_body(block, C, x_in, x_out, w1t, w2t, f, xh_marker, tern_done)
        if C.dbg:
            @block.gpsimd
            def _(e):
                e = EngWrap(e, C.chain["gpsimd"])
                e.wait_ge(C.s["st"].h, C.s["st"].v)
                e.dma_start(out=C.dbg["o_w1t"], in_=w1t).then_inc(C.s["g3"].inc(16), 16)
                e.dma_start(out=C.dbg["o_H"], in_=f["H"]).then_inc(C.s["g3"].inc(16), 16)
                e.wait_ge(C.s["g3"].h, C.s["g3"].v)
        emit_final_wait(block, C)
    return nc



_PROG_CACHE = {}


def _programs():
    if not _PROG_CACHE:
        _PROG_CACHE["ssm"] = build_ssm_program()
        _PROG_CACHE["moe"] = build_moe_program()
        _PROG_CACHE["dattn"] = build_dattn_program()
        _PROG_CACHE["gqa"] = build_gqa_program()
        _PROG_CACHE["l1ffn"] = build_l1_ffn_program(detect_races=False)
        _PROG_CACHE["gla"] = build_gla_program()
    return _PROG_CACHE


def kernel_unfused(**inputs):
    P = _programs()
    inp = inputs
    x = np.asarray(inp["L0_attn_a_x"], np.float32)
    consts = host_consts()
    moe_w = {
        "router": np.asarray(inp["L0_ffn_router_w"], np.float32),
        "gate": np.asarray(inp["L0_ffn_gate_proj"], np.float32),
        "up": np.asarray(inp["L0_ffn_up_proj"], np.float32),
        "down": np.asarray(inp["L0_ffn_down_proj"], np.float32),
        **consts,
    }
    ffn_w = {"w_fc": np.asarray(inp["L1_ffn_W_fc"], np.float32),
             "w_proj": np.asarray(inp["L1_ffn_W_proj"], np.float32)}

    def run(nc, per_core):
        sh = shard_x(x)
        maps = [{"x_in": sh[c], **per_core(c)} for c in range(NCORE)]
        res = run_bass_kernel_spmd(nc, maps, core_ids=list(range(NCORE)))
        return unshard_x([res.results[c]["x_out"] for c in range(NCORE)])

    x = run(P["ssm"], lambda c: ssm_host_inputs(inp, c))
    x = run(P["moe"], lambda c: moe_w)
    x = run(P["dattn"], lambda c: dattn_host_inputs(inp, c))
    x = run(P["moe"], lambda c: moe_w)
    x = run(P["gqa"], lambda c: gqa_host_inputs(inp, c))
    x = run(P["l1ffn"], lambda c: ffn_w)
    x = run(P["gla"], lambda c: gla_host_inputs(inp, c))
    x = run(P["l1ffn"], lambda c: ffn_w)
    return x



def emit_moe(block, C, x_dram, xout_dram, router_dram, gate_dram, up_dram,
             down_dram, m, xh_marker):
    Cs = dict(C.s)
    Cs["g3"] = C.s["ga3"]
    Cs["g4"] = C.s["ga4"]
    lw, lw2, lxr, mm, mm2, mm3, ev, ev2, g1, g3, g4, xh = (
        Cs[n] for n in ("lw", "lw2", "lxr", "mm", "mm2", "mm3", "ev", "ev2",
                         "g1", "g3", "g4", "xh"))
    b = {s: Cs[s].v for s in ("lw", "lw2", "lxr", "mm", "mm2", "mm3", "ev",
                               "ev2", "g1", "g3", "g4", "xh")}
    BIG = 60000.0
    pb7b = C.pb[7].ap().bitcast(BF16)

    @block.sync
    def _(e):
        e.dma_start(out=xout_dram, in_=x_dram).then_inc(lxr.h, 16)
    lxr.v += 16

    @block.gpsimd
    def _(e):
        e = EngWrap(e, C.chain["gpsimd"])
        for k in range(KC):
            e.dma_start(out=m["rt"][:, k, :],
                        in_=router_dram[k * 128:(k + 1) * 128, :]).then_inc(lw.h, 16)
    lw.v += 16 * KC

    @block.tensor
    def _(e):
        e.wait_ge(C.s["g2"].h, xh_marker + 4)
        e.wait_ge(lw.h, b["lw"] + 16 * KC)
        for t1 in range(KC):
            for k in range(KC):
                ins = e.matmul(C.pb[6][:, t1 * 8:(t1 + 1) * 8],
                               C.XH[:, k, t1 * 128:(t1 + 1) * 128],
                               m["rt"][:, k, :],
                               start=(k == 0), stop=(k == KC - 1),
                               skip_group_check=True)
        ins.then_inc(mm.h, 1)
    mm.v += 1

    @block.scalar
    def _(e):
        e = EngWrap(e, C.chain["scalar"])
        e.wait_ge(ev.h, b["ev"] + 1)
        e.activation(m["es"][:], m["dsb"][:], AF.Exp).then_inc(ev2.h, 1)
    ev2.v += 1

    @block.vector
    def _(e):
        e = EngWrap(e, C.chain["vector"])
        e.wait_ge(mm.h, b["mm"] + 1)
        s3 = m["ssb"].rearrange("p (t ee) -> p t ee", ee=8)
        sE = m["ssb"].rearrange("p (t ee) -> p ee t", ee=8)
        dE = m["dsb"].rearrange("p (t ee) -> p ee t", ee=8)
        s2E = m["s2"].rearrange("p (t ee) -> p ee t", ee=8)
        wEv = m["es"].rearrange("p (t ee) -> p ee t", ee=8)
        m1E = m["m1"].rearrange("p (o t) -> p o t", o=1).broadcast_to((128, 8, KC))
        m2E = m["m2"].rearrange("p (o t) -> p o t", o=1).broadcast_to((128, 8, KC))
        dnE = m["den"].rearrange("p (o t) -> p o t", o=1).broadcast_to((128, 8, KC))
        s23 = m["s2"].rearrange("p (t ee) -> p t ee", ee=8)
        w3 = m["es"].rearrange("p (t ee) -> p t ee", ee=8)
        e.tensor_copy(m["ssb"][:], C.pb[6][:, 0:128])
        e.tensor_reduce(m["m1"][:], s3, axis=AX.X, op=ALU.max, negate=True)
        e.tensor_tensor(dE, sE, m1E, op=ALU.add).then_inc(ev.h, 1)
        e.tensor_single_scalar(m["keep"][:], m["dsb"][:], 0.0, op=ALU.is_ge)
        e.scalar_tensor_tensor(m["s2"][:], m["keep"][:], -BIG, m["dsb"][:],
                               op0=ALU.mult, op1=ALU.add)
        e.tensor_reduce(m["m2"][:], s23, axis=AX.X, op=ALU.max, negate=True)
        e.tensor_tensor(s2E, dE, m2E, op=ALU.add)
        e.tensor_single_scalar(m["keep"][:], m["s2"][:], 0.0, op=ALU.is_ge)
        e.wait_ge(ev2.h, b["ev2"] + 1)
        e.tensor_tensor(m["es"][:], m["es"][:], m["keep"][:], op=ALU.mult)
        e.tensor_reduce(m["den"][:], w3, axis=AX.X, op=ALU.add)
        e.reciprocal(m["den"][:], m["den"][:])
        e.tensor_tensor(wEv, wEv, dnE, op=ALU.mult)
        e.tensor_copy(m["wr"][:], m["es"][:]).then_inc(g1.h, 1)
    ev.v += 1
    g1.v += 1

    @block.tensor
    def _(e):
        e.wait_ge(g1.h, b["g1"] + 1)
        e.wait_ge(C.s["aux"].h, getattr(C, "aux_need", 2))
        for t1 in range(KC):
            if t1 >= 1:
                e.wait_ge(ev2.h, b["ev2"] + 1 + t1)
            ins = e.transpose(pb7b[0:8, 0:128],
                              m["wr"][:, t1 * 8:(t1 + 1) * 8],
                              m["c_ident"][:])
            ins.then_inc(mm2.h, 1)
    mm2.v += KC

    @block.scalar
    def _(e):
        e = EngWrap(e, C.chain["scalar"])
        for t1 in range(KC):
            e.wait_ge(mm2.h, b["mm2"] + t1 + 1)
            e.copy(m["wfm8"][0:8, t1 * 128:(t1 + 1) * 128],
                   pb7b[0:8, 0:128]).then_inc(ev2.h, 1)
    ev2.v += KC

    for ex in range(8):
        be = {s: Cs[s].v for s in ("lw", "mm", "mm2", "mm3", "ev", "ev2",
                                    "g1", "g3", "g4", "xh")}

        @block.tensor
        def _(e):
            e.wait_ge(ev2.h, be["ev2"])
            e.wait_ge(C.s["aux"].h, getattr(C, "aux_need", 2))
            if ex >= 1:
                e.wait_ge(g1.h, be["g1"])
            for tt in range(4):
                ins = e.matmul(C.pb[6][:], m["c_esel"][0:8, ex * 128:(ex + 1) * 128],
                               m["wfm8"][0:8, tt * 512:(tt + 1) * 512],
                               start=True, stop=True, skip_group_check=True)
                ins.then_inc(mm2.h, 1)
                e.wait_ge(ev2.h, be["ev2"] + tt + 1)
        mm2.v += 4

        @block.scalar
        def _(e):
            e = EngWrap(e, C.chain["scalar"])
            for tt in range(4):
                e.wait_ge(mm2.h, be["mm2"] + tt + 1)
                e.copy(m["wbe"][:, tt * 512:(tt + 1) * 512], C.pb[6][:]
                       ).then_inc(ev2.h, 1)
        ev2.v += 4

        for fh in range(2):
            bu = {s: C.s[s].v for s in ("lw", "mm", "ev", "g1", "xh")}

            @block.gpsimd
            def _(e):
                e = EngWrap(e, C.chain["gpsimd"])
                if ex == 0 and fh == 0:
                    e.wait_ge(mm.h, b["mm"])
                else:
                    e.wait_ge(mm.h, bu["mm"])
                for k in range(KC):
                    e.dma_start(out=m["gw"][:, k, :],
                                in_=gate_dram[ex, k * 128:(k + 1) * 128,
                                              fh * 512:(fh + 1) * 512]
                                ).then_inc(lw.h, 16)
                for k in range(KC):
                    e.dma_start(out=m["uw"][:, k, :],
                                in_=up_dram[ex, k * 128:(k + 1) * 128,
                                            fh * 512:(fh + 1) * 512]
                                ).then_inc(lw.h, 16)
            lw.v += 32 * 16

            @block.tensor
            def _(e):
                e.wait_ge(lw.h, bu["lw"] + 16 * 32)
                for u in range(16):
                    fc, tt = u // 4, u % 4
                    pg = C.pb[u % 2]
                    pu = C.pb[2 + u % 2]
                    if u >= 2:
                        e.wait_ge(ev.h, bu["ev"] + u - 1)
                        e.wait_ge(g1.h, bu["g1"] + u - 1)
                    for k in range(KC):
                        e.matmul(pg[:], m["gw"][:, k, fc * 128:(fc + 1) * 128],
                                 C.XH[:, k, tt * 512:(tt + 1) * 512],
                                 start=(k == 0), stop=(k == KC - 1),
                                 skip_group_check=True)
                    for k in range(KC):
                        ins = e.matmul(pu[:], m["uw"][:, k, fc * 128:(fc + 1) * 128],
                                       C.XH[:, k, tt * 512:(tt + 1) * 512],
                                       start=(k == 0), stop=(k == KC - 1),
                                       skip_group_check=True)
                    ins.then_inc(mm.h, 1)
            mm.v += 16

            @block.scalar
            def _(e):
                e = EngWrap(e, C.chain["scalar"])
                for u in range(16):
                    e.wait_ge(mm.h, bu["mm"] + u + 1)
                    if u >= 2:
                        e.wait_ge(xh.h, bu["xh"] + u - 1)
                    e.activation(m["sg"][:, u % 2, :], C.pb[u % 2][:],
                                 AF.Silu).then_inc(ev.h, 1)
            ev.v += 16

            @block.vector
            def _(e):
                e = EngWrap(e, C.chain["vector"])
                for u in range(16):
                    fc, tt = u // 4, u % 4
                    e.wait_ge(mm.h, bu["mm"] + u + 1)
                    e.tensor_tensor(m["uwt"][:, u % 2, :], C.pb[2 + u % 2][:],
                                    m["wbe"][:, tt * 512:(tt + 1) * 512],
                                    op=ALU.mult).then_inc(g1.h, 1)
                    e.wait_ge(ev.h, bu["ev"] + u + 1)
                    e.tensor_tensor(m["Hp"][:, fh * 4 + fc,
                                            tt * 512:(tt + 1) * 512],
                                    m["sg"][:, u % 2, :], m["uwt"][:, u % 2, :],
                                    op=ALU.mult).then_inc(xh.h, 1)
            g1.v += 16
            xh.v += 16

        for dh in range(2):
            bd = {s: Cs[s].v for s in ("lw2", "mm3", "ev2", "g3", "g4", "xh")}

            @block.gpsimd
            def _(e):
                e = EngWrap(e, C.chain["gpsimd"])
                e.wait_ge(xh.h, be["xh"] + 32)
                e.wait_ge(mm3.h, bd["mm3"])
                for fc in range(8):
                    e.dma_start(out=m["dw"][:, fc, :],
                                in_=down_dram[ex, fc * 128:(fc + 1) * 128,
                                              dh * 1024:(dh + 1) * 1024]
                                ).then_inc(lw2.h, 16)
                for j in range(32):
                    e.wait_ge(ev2.h, bd["ev2"] + j + 1)
                    tt, dc = j // 8, j % 8
                    gdc = dh * 8 + dc
                    psem = g3 if j % 2 == 0 else g4
                    e.dma_start(out=xout_dram[gdc * 128:(gdc + 1) * 128,
                                              tt * 512:(tt + 1) * 512],
                                in_=m["yev"][:, j % 2, :],
                                accum_op=ALU.add).then_inc(psem.h, 16)
            lw2.v += 16 * 8
            g3.v += 16 * 16
            g4.v += 16 * 16

            @block.tensor
            def _(e):
                if ex == 0 and dh == 0:
                    e.wait_ge(lxr.h, b["lxr"] + 16)
                e.wait_ge(lw2.h, bd["lw2"] + 16 * 8)
                e.wait_ge(xh.h, be["xh"] + 32)
                for tt in range(4):
                    if tt >= 1:
                        e.wait_ge(ev2.h, bd["ev2"] + tt * 8)
                    for fc in range(8):
                        for dc in range(8):
                            ins = e.matmul(C.pb[dc][:],
                                           m["dw"][:, fc, dc * 128:(dc + 1) * 128],
                                           m["Hp"][:, fc, tt * 512:(tt + 1) * 512],
                                           start=(fc == 0), stop=(fc == 7),
                                           skip_group_check=True)
                            if fc == 7:
                                ins.then_inc(mm3.h, 1)
            mm3.v += 32

            @block.scalar
            def _(e):
                e = EngWrap(e, C.chain["scalar"])
                for j in range(32):
                    tt, dc = j // 8, j % 8
                    e.wait_ge(mm3.h, bd["mm3"] + tt * 8 + dc + 1)
                    if j >= 2:
                        psem, pb0 = (g3, bd["g3"]) if j % 2 == 0 else (g4, bd["g4"])
                        e.wait_ge(psem.h, pb0 + 16 * ((j - 2) // 2 + 1))
                    e.copy(m["yev"][:, j % 2, :], C.pb[dc][:]).then_inc(ev2.h, 1)
            ev2.v += 32

    return b["g3"]


def moe_tiles(nc, ctx, pfx="m_"):
    m = {
        "rt": ctx.enter_context(nc.sbuf_tensor(pfx + "rt", [128, KC, 8], BF16)).ap(),
        "ssb": ctx.enter_context(nc.sbuf_tensor(pfx + "ssb", [128, 128], F32)).ap(),
        "dsb": ctx.enter_context(nc.sbuf_tensor(pfx + "dsb", [128, 128], F32)).ap(),
        "s2": ctx.enter_context(nc.sbuf_tensor(pfx + "s2", [128, 128], F32)).ap(),
        "es": ctx.enter_context(nc.sbuf_tensor(pfx + "es", [128, 128], F32)).ap(),
        "keep": ctx.enter_context(nc.sbuf_tensor(pfx + "keep", [128, 128], F32)).ap(),
        "m1": ctx.enter_context(nc.sbuf_tensor(pfx + "m1", [128, KC], F32)).ap(),
        "m2": ctx.enter_context(nc.sbuf_tensor(pfx + "m2", [128, KC], F32)).ap(),
        "den": ctx.enter_context(nc.sbuf_tensor(pfx + "den", [128, KC], F32)).ap(),
        "wr": ctx.enter_context(nc.sbuf_tensor(pfx + "wr", [128, 128], BF16)).ap(),
        "wfm8": ctx.enter_context(nc.sbuf_tensor(pfx + "wfm8", [32, 2048], BF16)).ap(),
        "wbe": ctx.enter_context(nc.sbuf_tensor(pfx + "wbe", [128, 2048], F32)).ap(),
        "gw": ctx.enter_context(nc.sbuf_tensor(pfx + "gw", [128, KC, 512], BF16)).ap(),
        "uw": ctx.enter_context(nc.sbuf_tensor(pfx + "uw", [128, KC, 512], BF16)).ap(),
        "dw": ctx.enter_context(nc.sbuf_tensor(pfx + "dw", [128, 8, 1024], BF16)).ap(),
        "sg": ctx.enter_context(nc.sbuf_tensor(pfx + "sg", [128, 2, 512], BF16)).ap(),
        "uwt": ctx.enter_context(nc.sbuf_tensor(pfx + "uwt", [128, 2, 512], BF16)).ap(),
        "Hp": ctx.enter_context(nc.sbuf_tensor(pfx + "Hp", [128, 8, TOK], BF16)).ap(),
        "yev": ctx.enter_context(nc.sbuf_tensor(pfx + "yev", [128, 2, 512], F32)).ap(),
    }
    return m


def emit_load_consts(block, C, cdram):
    @block.gpsimd
    def _(e):
        e = EngWrap(e, C.chain["gpsimd"])
        e.dma_start(out=C.c_ident[:], in_=cdram["c_ident"]).then_inc(C.s["lw"].inc(16), 16)
        e.dma_start(out=C.c_esel[:], in_=cdram["c_esel"]).then_inc(C.s["lw"].inc(16), 16)
        e.wait_ge(C.s["lw"].h, C.s["lw"].v)
        e.sem_inc(C.s["aux"].h, 1)
    C.s["aux"].v += 1


def host_consts():
    import ml_dtypes
    ident = np.eye(128, dtype=ml_dtypes.bfloat16)
    esel = np.zeros((8, 1024), dtype=ml_dtypes.bfloat16)
    for ee in range(8):
        esel[ee, ee * 128:(ee + 1) * 128] = 1.0
    return {"c_ident": ident, "c_esel": esel}


def build_moe_program():
    nc = bass.Bass("TRN2", target_bir_lowering=False, debug=False,
                   detect_race_conditions=False)
    x_in = nc.dram_tensor("x_in", [D, TOK], F32, kind="ExternalInput").ap()
    router = nc.dram_tensor("router", [D, 8], F32, kind="ExternalInput").ap()
    gate = nc.dram_tensor("gate", [8, D, 1024], F32, kind="ExternalInput").ap()
    up = nc.dram_tensor("up", [8, D, 1024], F32, kind="ExternalInput").ap()
    down = nc.dram_tensor("down", [8, 1024, D], F32, kind="ExternalInput").ap()
    c_ident = nc.dram_tensor("c_ident", [128, 128], BF16, kind="ExternalInput").ap()
    c_esel = nc.dram_tensor("c_esel", [8, 1024], BF16, kind="ExternalInput").ap()
    x_out = nc.dram_tensor("x_out", [D, TOK], F32, kind="ExternalOutput").ap()

    with contextlib.ExitStack() as ctx:
        C = Ctx(nc, ctx)
        C.c_ident = ctx.enter_context(nc.sbuf_tensor("c_ident_sb", [128, 128], BF16))
        C.c_esel = ctx.enter_context(nc.sbuf_tensor("c_esel_sb", [8, 1024], BF16))
        m = moe_tiles(nc, ctx)
        m["c_ident"] = C.c_ident.ap()
        m["c_esel"] = C.c_esel.ap()
        block = ctx.enter_context(nc.Block())
        emit_consts(block, C)
        emit_load_consts(block, C, {"c_ident": c_ident, "c_esel": c_esel})
        xh_marker = emit_ln(block, C, x_in)
        emit_moe(block, C, x_in, x_out, router, gate, up, down, m, xh_marker)
        emit_final_wait(block, C)
    return nc




def emit_outproj(block, C, x_dram, xout_dram, wo_ap, a, ag, cc_a, bo=None):
    (lw2, ls1, lxr, mm, mm3, ev, ev2, g1, st, st2, cc, xh) = (
        C.s[n] for n in ("lw2", "ls1", "lxr", "mm", "mm3", "ev", "ev2", "g1",
                         "st", "st2", "cc", "xh"))
    bo = {n: C.s[n].v for n in C.SEMS}

    def catchup(e, *names, at):
        for n in names:
            e.wait_ge(C.s[n].h, at[n])

    @block.gpsimd
    def _(e):
        e = EngWrap(e, C.chain["gpsimd"])
        e.wait_ge(mm.h, bo["mm"])
        for hc in range(KC):
            e.dma_start(out=a["wos"][:, hc, :],
                        in_=wo_ap[hc * 128:(hc + 1) * 128, :]).then_inc(lw2.h, 16)

    @block.sync
    def _(e):
        e.wait_ge(cc.h, cc_a)
        for tt in range(4):
            for hc in range(KC):
                hg = tt * KC + hc
                if hg >= 4:
                    e.wait_ge(g1.h, bo["g1"] + hg - 3)
                for hf in range(2):
                    e.dma_start(
                        out=a["ats"][:, hg % 4, hf, :],
                        in_=ag["a_all"][hc * 128:(hc + 1) * 128,
                                        hf * 2048 + tt * 512: hf * 2048 + (tt + 1) * 512],
                    ).then_inc(ls1.h, 16)
            for j in range(KC):
                gdc = j
                jj = tt * KC + j
                if jj >= 2:
                    e.wait_ge(xh.h, bo["xh"] + jj - 1)
                e.dma_start(out=a["xr"][:, jj % 2, :],
                            in_=x_dram[gdc * 128:(gdc + 1) * 128,
                                       tt * 512:(tt + 1) * 512]).then_inc(lxr.h, 16)
                e.wait_ge(xh.h, bo["xh"] + jj + 1)
                e.dma_start(out=xout_dram[gdc * 128:(gdc + 1) * 128,
                                          tt * 512:(tt + 1) * 512],
                            in_=a["yt"][:, jj % 2, :]
                            ).then_inc((st if jj % 2 == 0 else st2).h, 16)
    lw2.v += 16 * KC
    ls1.v += 16 * 4 * KC * 2
    lxr.v += 16 * 64
    st.v += 16 * 32
    st2.v += 16 * 32

    @block.vector
    def _(e):
        e = EngWrap(e, C.chain["vector"])
        for tt in range(4):
            if tt >= 1:
                e.wait_ge(mm3.h, bo["mm3"] + tt * 16)
            for hc in range(KC):
                hg = tt * KC + hc
                e.wait_ge(ls1.h, bo["ls1"] + 16 * 2 * min(hg + 4, 16 * (tt + 1)))
                e.tensor_scalar_mul(a["atc"][:, hc, :], a["ats"][:, hg % 4, 0, :],
                                    a["sel"][:, 0:1])
                ins = e.scalar_tensor_tensor(a["atc"][:, hc, :],
                                             a["ats"][:, hg % 4, 1, :],
                                             a["sel"][:, 1:2],
                                             a["atc"][:, hc, :],
                                             op0=ALU.mult, op1=ALU.add)
                ins.then_inc(g1.h, 1)
            for j in range(KC):
                jj = tt * KC + j
                dc = j % 8
                e.wait_ge(mm3.h, bo["mm3"] + jj + 1)
                e.wait_ge(lxr.h, bo["lxr"] + 16 * (jj + 1))
                if jj >= 2:
                    psem, pb0 = (st, bo["st"]) if jj % 2 == 0 else (st2, bo["st2"])
                    e.wait_ge(psem.h, pb0 + 16 * ((jj - 2) // 2 + 1))
                ins = e.tensor_tensor(a["yt"][:, jj % 2, :], C.pb[dc][:],
                                      a["xr"][:, jj % 2, :], op=ALU.add)
                ins.then_inc(xh.h, 1)
    g1.v += 4 * KC
    xh.v += 64

    @block.tensor
    def _(e):
        catchup(e, "ev", "ev2", at=bo)
        e.wait_ge(lw2.h, bo["lw2"] + 16 * KC)
        for tt in range(4):
            for dcb in range(2):
                jround = tt * 2 + dcb
                if jround >= 1:
                    e.wait_ge(xh.h, bo["xh"] + jround * 8)
                e.wait_ge(g1.h, bo["g1"] + (tt + 1) * KC)
                for hc in range(KC):
                    for dc in range(8):
                        gdc = dcb * 8 + dc
                        ins = e.matmul(C.pb[dc][:],
                                       a["wos"][:, hc, gdc * 128:(gdc + 1) * 128],
                                       a["atc"][:, hc, :],
                                       start=(hc == 0), stop=(hc == KC - 1),
                                       skip_group_check=True)
                        if hc == KC - 1:
                            ins.then_inc(mm3.h, 1)
    mm3.v += 64
    return



def emit_gqa(block, C, x_dram, xout_dram, W, a, xh_marker, ag):
    (lx, lw, lw2, ls1, lxr, mm, mm2, mm3, ev, ev2,
     g1, g3, g4, st, st2, cc, xh, aux) = (
        C.s[n] for n in ("lx", "lw", "lw2", "ls1", "lxr", "mm", "mm2", "mm3",
                         "ev", "ev2", "g1", "g3", "g4", "st", "st2", "cc",
                         "xh", "aux"))
    b = {n: C.s[n].v for n in C.SEMS}
    pb4b = C.pb[4].ap().bitcast(BF16)

    def catchup(e, *names, at=None):
        for n in names:
            e.wait_ge(C.s[n].h, (at or {n: C.s[n].v for n in C.SEMS})[n])

    @block.gpsimd
    def _(e):
        e = EngWrap(e, C.chain["gpsimd"])
        e.wait_ge(C.s["g2"].h, xh_marker + 4)
        for k in range(KC):
            e.dma_start(out=ag["x_my"][k * 128:(k + 1) * 128, :],
                        in_=C.XH[:, k, :]).then_inc(g3.h, 16)
        e.wait_ge(g3.h, b["g3"] + 16 * KC)
        e.collective_compute(
            "AllGather", ALU.bypass,
            replica_groups=[[0, 1], [2, 3], [4, 5], [6, 7]],
            ins=[ag["x_my"]], outs=[ag["x_all"]],
        ).then_inc(cc.h, 1)
    g3.v += 16 * KC
    cc.v += 1
    cc_xhat = cc.v

    @block.gpsimd
    def _(e):
        e = EngWrap(e, C.chain["gpsimd"])
        for k in range(KC):
            e.dma_start(out=a["wkv_s"][:, k, :],
                        in_=W["wkv"][k * 128:(k + 1) * 128, :]).then_inc(lw.h, 16)
        e.dma_start(out=a["msk"][:], in_=W["c_mask4"].rearrange(
            "(mi p) f -> p mi f", p=128)).then_inc(lw.h, 16)
        e.dma_start(out=a["rot"][:], in_=W["c_rot"]).then_inc(lw.h, 16)
        e.dma_start(out=a["sel"][:], in_=W["c_sel"]).then_inc(lw.h, 16)
        e.dma_start(out=a["identb"][:], in_=W["c_ident"]).then_inc(lw.h, 16)
    lw.v += 16 * (KC + 4)
    lw_static = lw.v

    ntile = [0]

    for th in range(2):
        bt = {n: C.s[n].v for n in C.SEMS}

        @block.sync
        def _(e):
            e.wait_ge(cc.h, cc_xhat)
            if th == 1:
                e.wait_ge(mm.h, bt["mm"])
            for k in range(KC):
                e.dma_start(
                    out=C.XH[:, k, :],
                    in_=ag["x_all"][th * 2048 + k * 128: th * 2048 + (k + 1) * 128, :],
                ).then_inc(ls1.h, 16)
        ls1.v += 16 * KC

        @block.gpsimd
        def _(e):
            e = EngWrap(e, C.chain["gpsimd"])
            if th == 1:
                e.wait_ge(mm.h, bt["mm"])
            e.dma_start(out=a["ct"][:], in_=W["costm"][th * 2048:(th + 1) * 2048, :]
                        .rearrange("(t p) d -> p t d", p=128)).then_inc(lw.h, 16)
            e.dma_start(out=a["stm"][:], in_=W["sintm"][th * 2048:(th + 1) * 2048, :]
                        .rearrange("(t p) d -> p t d", p=128)).then_inc(lw.h, 16)
            e.dma_start(out=a["cq"][:], in_=W["cosq"][:, th * 2048:(th + 1) * 2048]
                        ).then_inc(lw.h, 16)
            e.dma_start(out=a["sq"][:], in_=W["sinq"][:, th * 2048:(th + 1) * 2048]
                        ).then_inc(lw.h, 16)
        lw.v += 16 * 4
        lw_half = lw.v

        @block.tensor
        def _(e):
            catchup(e, "ev", "ev2", "g1", "xh", at=bt)
            e.wait_ge(ls1.h, bt["ls1"] + 16 * KC)
            e.wait_ge(lw.h, lw_half)
            for t in range(KC):
                pkv = C.pb[t % 2]
                if t >= 2:
                    e.wait_ge(ev.h, bt["ev"] + 2 * (t - 1))
                for k in range(KC):
                    ins = e.matmul(pkv[:], C.XH[:, k, t * 128:(t + 1) * 128],
                                   a["wkv_s"][:, k, :],
                                   start=(k == 0), stop=(k == KC - 1),
                                   skip_group_check=True)
                ins.then_inc(mm.h, 1)
                if t >= 1:
                    for kv in range(2):
                        j = 2 * (t - 1) + kv
                        e.wait_ge(g1.h, bt["g1"] + t)
                        if j >= 1:
                            e.wait_ge(ev2.h, bt["ev2"] + j)
                        ins = e.transpose(pb4b[0:128, 0:128],
                                          a["krt"][:, (t - 1) % 2,
                                                   kv * 128:(kv + 1) * 128],
                                          a["identb"][:])
                        ins.then_inc(mm2.h, 1)
            e.wait_ge(g1.h, bt["g1"] + KC)
            for kv in range(2):
                j = 2 * (KC - 1) + kv
                e.wait_ge(ev2.h, bt["ev2"] + j)
                ins = e.transpose(pb4b[0:128, 0:128],
                                  a["krt"][:, (KC - 1) % 2, kv * 128:(kv + 1) * 128],
                                  a["identb"][:])
                ins.then_inc(mm2.h, 1)
        mm.v += KC
        mm2.v += 2 * KC

        @block.scalar
        def _(e):
            e = EngWrap(e, C.chain["scalar"])
            for t in range(KC + 1):
                if t < KC:
                    gt = th * KC + t
                    e.wait_ge(mm.h, bt["mm"] + t + 1)
                    if t >= 2:
                        e.wait_ge(g1.h, bt["g1"] + t - 1)
                    e.copy(a["kvt"][:, t % 2, :], C.pb[t % 2][:]).then_inc(ev.h, 1)
                    e.copy(a["Vs"][:, gt, :], C.pb[t % 2][:, 256:512]
                           ).then_inc(ev.h, 1)
                if t >= 1:
                    for kv in range(2):
                        j = 2 * (t - 1) + kv
                        e.wait_ge(mm2.h, bt["mm2"] + j + 1)
                        e.copy(a["Kf"][:, kv,
                                       (th * KC + t - 1) * 128:(th * KC + t) * 128],
                               pb4b[0:128, 0:128]).then_inc(ev2.h, 1)
        ev.v += 2 * KC
        ev2.v += 2 * KC

        @block.vector
        def _(e):
            e = EngWrap(e, C.chain["vector"])
            t3 = a["tmpk"].rearrange("p (h d) -> p h d", d=64)
            for t in range(KC):
                e.wait_ge(ev.h, bt["ev"] + 2 * t + 2)
                if t >= 2:
                    e.wait_ge(mm2.h, bt["mm2"] + 2 * (t - 1))
                kv3 = a["kvt"][:, t % 2, 0:256].rearrange("p (h d) -> p h d", d=128)
                kr3 = a["krt"][:, t % 2, :].rearrange("p (h d) -> p h d", d=128)
                clo = a["ct"][:, t, 0:64].rearrange("p (o d) -> p o d", o=1
                                                    ).broadcast_to((128, 2, 64))
                slo = a["stm"][:, t, 0:64].rearrange("p (o d) -> p o d", o=1
                                                     ).broadcast_to((128, 2, 64))
                e.tensor_tensor(kr3[:, :, 0:64], kv3[:, :, 64:128], slo, op=ALU.mult)
                e.tensor_tensor(t3, kv3[:, :, 0:64], clo, op=ALU.mult)
                e.tensor_tensor(kr3[:, :, 0:64], t3, kr3[:, :, 0:64], op=ALU.subtract)
                e.tensor_tensor(kr3[:, :, 64:128], kv3[:, :, 64:128], clo, op=ALU.mult)
                e.tensor_tensor(t3, kv3[:, :, 0:64], slo, op=ALU.mult)
                ins = e.tensor_tensor(kr3[:, :, 64:128], kr3[:, :, 64:128], t3,
                                      op=ALU.add)
                ins.then_inc(g1.h, 1)
        g1.v += KC

        bn = {n: C.s[n].v for n in C.SEMS}

        @block.scalar
        def _(e):
            e = EngWrap(e, C.chain["scalar"])
            e.wait_ge(ev2.h, bn["ev2"])
            for kv in range(2):
                e.activation(a["sqk"][:], a["Kf"][:, kv, th * 2048:(th + 1) * 2048],
                             AF.Square).then_inc(ev2.h, 1)
                e.wait_ge(g1.h, bn["g1"] + kv + 1)
        ev2.v += 2

        @block.tensor
        def _(e):
            catchup(e, "ev", "g1", "xh", at=bn)
            for kv in range(2):
                e.wait_ge(ev2.h, bn["ev2"] + kv + 1)
                for tt in range(4):
                    ins = e.matmul(C.pb[5][:], C.ones_bf[:],
                                   a["sqk"][:, tt * 512:(tt + 1) * 512],
                                   start=(tt == 0), stop=(tt == 3),
                                   skip_group_check=True)
                ins.then_inc(mm3.h, 1)
                e.wait_ge(g1.h, bn["g1"] + kv + 1)
        mm3.v += 2

        @block.vector
        def _(e):
            e = EngWrap(e, C.chain["vector"])
            for kv in range(2):
                e.wait_ge(mm3.h, bn["mm3"] + kv + 1)
                e.reduce_max(a["n2b"][:], C.pb[5][:], axis=AX.X)
                if th == 0:
                    ins = e.tensor_copy(a["nk2"][:, kv:kv + 1], a["n2b"][:])
                else:
                    ins = e.tensor_tensor(a["nk2"][:, kv:kv + 1],
                                          a["nk2"][:, kv:kv + 1], a["n2b"][:],
                                          op=ALU.max)
                ins.then_inc(g1.h, 1)
        g1.v += 2

        for qh in range(8):
            bq = {n: C.s[n].v for n in C.SEMS}
            kvh = qh // 4

            @block.gpsimd
            def _(e):
                e = EngWrap(e, C.chain["gpsimd"])
                if not (th == 0 and qh == 0):
                    e.wait_ge(mm.h, bq["mm"])
                for k in range(KC):
                    e.dma_start(out=a["wqs"][:, k, :],
                                in_=W["wq"][k * 128:(k + 1) * 128,
                                            qh * 128:(qh + 1) * 128]
                                ).then_inc(lw.h, 16)
            lw.v += 16 * KC

            @block.tensor
            def _(e):
                catchup(e, "ev", "ev2", "g1", "xh", at=bq)
                e.wait_ge(lw.h, bq["lw"] + 16 * KC)
                for tt in range(4):
                    pq = C.pb[tt % 2]
                    if tt >= 2:
                        e.wait_ge(ev.h, bq["ev"] + tt - 1)
                    for k in range(KC):
                        ins = e.matmul(pq[:], a["wqs"][:, k, :],
                                       C.XH[:, k, tt * 512:(tt + 1) * 512],
                                       start=(k == 0), stop=(k == KC - 1),
                                       skip_group_check=True)
                    ins.then_inc(mm.h, 1)
                for tt in range(4):
                    e.wait_ge(ev.h, bq["ev"] + tt + 1)
                    if tt >= 2:
                        e.wait_ge(g1.h, bq["g1"] + tt - 1)
                    ins = e.matmul(C.pb[2 + tt % 2][:], a["rot"][:],
                                   a["qsb"][:, tt * 512:(tt + 1) * 512],
                                   start=True, stop=True, skip_group_check=True)
                    ins.then_inc(mm2.h, 1)
            mm.v += 4
            mm2.v += 4

            @block.scalar
            def _(e):
                e = EngWrap(e, C.chain["scalar"])
                for tt in range(4):
                    e.wait_ge(mm.h, bq["mm"] + tt + 1)
                    e.copy(a["qsb"][:, tt * 512:(tt + 1) * 512],
                           C.pb[tt % 2][:]).then_inc(ev.h, 1)
                e.wait_ge(g1.h, bq["g1"] + 4)
                e.activation(a["sqk"][:], a["qr"][:], AF.Square).then_inc(ev2.h, 1)
            ev.v += 4
            ev2.v += 1

            @block.vector
            def _(e):
                e = EngWrap(e, C.chain["vector"])
                for tt in range(4):
                    sl = slice(tt * 512, (tt + 1) * 512)
                    e.wait_ge(mm2.h, bq["mm2"] + tt + 1)
                    e.tensor_tensor(a["qr"][:, sl], a["qsb"][:, sl],
                                    a["cq"][:, sl], op=ALU.mult)
                    e.tensor_tensor(a["tq2"][:], C.pb[2 + tt % 2][:],
                                    a["sq"][:, sl], op=ALU.mult)
                    ins = e.tensor_tensor(a["qr"][:, sl], a["qr"][:, sl],
                                          a["tq2"][:], op=ALU.add)
                    ins.then_inc(g1.h, 1)
            g1.v += 4

            @block.tensor
            def _(e):
                e.wait_ge(ev2.h, bq["ev2"] + 1)
                e.wait_ge(g1.h, bq["g1"] + 4)
                for tt in range(4):
                    ins = e.matmul(C.pb[5][:], C.ones_bf[:],
                                   a["sqk"][:, tt * 512:(tt + 1) * 512],
                                   start=(tt == 0), stop=(tt == 3),
                                   skip_group_check=True)
                ins.then_inc(mm3.h, 1)
            mm3.v += 1

            @block.vector
            def _(e):
                e = EngWrap(e, C.chain["vector"])
                e.wait_ge(mm3.h, bq["mm3"] + 1)
                e.reduce_max(a["n2b"][:], C.pb[5][:], axis=AX.X)
                ins = e.tensor_tensor(a["n2b"][:], a["n2b"][:],
                                      a["nk2"][:, kvh:kvh + 1], op=ALU.mult)
                ins.then_inc(ev.h, 1)
            ev.v += 1

            @block.scalar
            def _(e):
                e = EngWrap(e, C.chain["scalar"])
                e.wait_ge(ev.h, bq["ev"] + 5)
                e.activation(a["nC"][:, qh:qh + 1], a["n2b"][:], AF.Sqrt)
                e.mul(a["nC"][:, qh:qh + 1], a["nC"][:, qh:qh + 1], -1.0
                      ).then_inc(ev2.h, 1)
            ev2.v += 1

            for qt in range(4):
                bf = {n: C.s[n].v for n in C.SEMS}
                ktn = th * KC + (qt + 1) * 4
                qt0 = th * 2048 + qt * 512
                nt = ntile[0]

                @block.tensor
                def _(e):
                    e.wait_ge(ev2.h, bq["ev2"] + 2)
                    e.wait_ge(xh.h, bf["xh"])
                    for kt in range(ktn + 1):
                        if kt < ktn:
                            ps = C.pb[kt % 2]
                            if kt >= 2:
                                e.wait_ge(ev.h, bf["ev"] + kt - 1)
                            ins = e.matmul(ps[:],
                                           a["Kf"][:, kvh, kt * 128:(kt + 1) * 128],
                                           a["qr"][:, qt * 512:(qt + 1) * 512],
                                           start=True, stop=True,
                                           skip_group_check=True)
                            ins.then_inc(mm.h, 1)
                        if kt >= 1:
                            pk = kt - 1
                            diag = (pk * 128 >= qt0)
                            if diag:
                                e.wait_ge(g1.h, bf["g1"] + (pk - (ktn - 4)) + 1)
                            else:
                                e.wait_ge(ev.h, bf["ev"] + pk + 1)
                            e.matmul(C.pb[2][:],
                                     a["Vs"][:, pk, kvh * 128:(kvh + 1) * 128],
                                     a["ptile"][:, pk % 2, :],
                                     start=(pk == 0), stop=(pk == ktn - 1),
                                     skip_group_check=True)
                            ins2 = e.matmul(C.pb[3][:], C.ones_bf[:],
                                            a["ptile"][:, pk % 2, :],
                                            start=(pk == 0), stop=(pk == ktn - 1),
                                            skip_group_check=True)
                            if pk == ktn - 1:
                                ins2.then_inc(mm2.h, 1)
                mm.v += ktn
                mm2.v += 1

                @block.scalar
                def _(e):
                    e = EngWrap(e, C.chain["scalar"])
                    for kt in range(ktn):
                        e.wait_ge(mm.h, bf["mm"] + kt + 1)
                        e.activation(a["ptile"][:, kt % 2, :], C.pb[kt % 2][:],
                                     AF.Exp, bias=a["nC"][:, qh:qh + 1]
                                     ).then_inc(ev.h, 1)
                ev.v += ktn

                @block.vector
                def _(e):
                    e = EngWrap(e, C.chain["vector"])
                    for kt in range(ktn - 4, ktn):
                        di = kt - (ktn - 4)
                        e.wait_ge(ev.h, bf["ev"] + kt + 1)
                        e.tensor_tensor(a["ptile"][:, kt % 2, :],
                                        a["ptile"][:, kt % 2, :],
                                        a["msk"][:, di, :],
                                        op=ALU.mult).then_inc(g1.h, 1)
                    e.wait_ge(mm2.h, bf["mm2"] + 1)
                    if nt >= 1:
                        e.wait_ge(g4.h, b["g4"] + 16 * nt)
                    e.reciprocal(a["rs"][:], C.pb[3][:])
                    ins = e.tensor_tensor(a["at"][:, nt % 2, :], C.pb[2][:],
                                          a["rs"][:], op=ALU.mult)
                    ins.then_inc(xh.h, 1)
                g1.v += 4
                xh.v += 1

                @block.gpsimd
                def _(e):
                    e = EngWrap(e, C.chain["gpsimd"])
                    e.wait_ge(xh.h, bf["xh"] + 1)
                    e.dma_start(out=ag["a_my"][qh * 128:(qh + 1) * 128,
                                               qt0:qt0 + 512],
                                in_=a["at"][:, nt % 2, :]).then_inc(g4.h, 16)
                g4.v += 16
                ntile[0] += 1

    @block.gpsimd
    def _(e):
        e = EngWrap(e, C.chain["gpsimd"])
        e.wait_ge(g4.h, b["g4"] + 16 * 64)
        e.collective_compute(
            "AllGather", ALU.bypass,
            replica_groups=[[0, 1], [2, 3], [4, 5], [6, 7]],
            ins=[ag["a_my"]], outs=[ag["a_all"]],
        ).then_inc(cc.h, 1)
    cc.v += 1
    cc_a = cc.v

    emit_outproj(block, C, x_dram, xout_dram, W["wo"], a, ag, cc_a)
    return b


def gqa_tiles(nc, ctx, C):
    a = {}
    def sb(name, shape, dt):
        a[name] = ctx.enter_context(nc.sbuf_tensor("a_" + name, shape, dt)).ap()
    sb("wkv_s", [128, KC, 512], BF16)
    sb("msk", [128, 4, 512], BF16)
    sb("rot", [128, 128], BF16)
    sb("sel", [128, 2], F32)
    sb("identb", [128, 128], BF16)
    sb("ct", [128, KC, 128], BF16)
    sb("stm", [128, KC, 128], BF16)
    sb("cq", [128, 2048], BF16)
    sb("sq", [128, 2048], BF16)
    sb("kvt", [128, 2, 512], BF16)
    sb("krt", [128, 2, 256], BF16)
    sb("tmpk", [128, 128], BF16)
    sb("Kf", [128, 2, 4096], BF16)
    sb("Vs", [128, 32, 256], BF16)
    sb("wqs", [128, KC, 128], BF16)
    sb("qsb", [128, 2048], BF16)
    sb("qr", [128, 2048], BF16)
    sb("tq2", [128, 512], BF16)
    sb("sqk", [128, 2048], BF16)
    sb("nC", [128, 8], F32)
    sb("n2b", [128, 1], F32)
    sb("nk2", [128, 2], F32)
    sb("ptile", [128, 2, 512], BF16)
    sb("rs", [128, 512], F32)
    sb("at", [128, 2, 512], BF16)
    sb("ats", [128, 4, 2, 512], BF16)
    a["atc"] = a["wkv_s"]
    sb("yt", [128, 2, 512], F32)
    sb("xr", [128, 2, 512], F32)
    a["wos"] = C.XH.ap().rearrange("p k t -> p k t")
    return a


def gqa_host_inputs(inp, core):
    import ml_dtypes
    g = core % 2
    wq = np.asarray(inp["L1_attn_a_wq"], np.float32)
    wk = np.asarray(inp["L1_attn_a_wk"], np.float32)
    wv = np.asarray(inp["L1_attn_a_wv"], np.float32)
    wo = np.asarray(inp["L1_attn_a_wo"], np.float32)
    wq_my = np.ascontiguousarray(wq[:, g * 1024:(g + 1) * 1024])
    wkv_my = np.ascontiguousarray(
        np.concatenate([wk[:, g * 256:(g + 1) * 256],
                        wv[:, g * 256:(g + 1) * 256]], axis=1))
    HD = 128
    freqs = 1.0 / 10000.0 ** (np.arange(0, HD, 2, dtype=np.float32) / HD)
    t = np.arange(SEQ, dtype=np.float32)
    ang = t[:, None] * freqs[None, :]
    cos = np.tile(np.cos(ang), (1, 2)).astype(np.float32)
    sin = np.tile(np.sin(ang), (1, 2)).astype(np.float32)
    s = np.float32(HD ** -0.5)
    cosq = np.ascontiguousarray(cos.T * s)
    sinq = np.ascontiguousarray(sin.T * s)
    masks = np.zeros((4, 128, 512), dtype=ml_dtypes.bfloat16)
    for di in range(4):
        f = np.arange(512)[None, :]
        p = np.arange(128)[:, None]
        masks[di] = (f >= p + 128 * di).astype(ml_dtypes.bfloat16)
    rotT = np.zeros((128, 128), dtype=ml_dtypes.bfloat16)
    for i in range(64):
        rotT[i + 64, i] = -1.0
        rotT[i, i + 64] = 1.0
    sel = np.zeros((128, 2), np.float32)
    sel[:, g] = 1.0
    import ml_dtypes as md
    return {
        "wq": wq_my, "wkv": wkv_my, "wo": wo,
        "cosq": cosq, "sinq": sinq,
        "costm": np.ascontiguousarray(cos), "sintm": np.ascontiguousarray(sin),
        "c_rot": rotT, "c_mask4": masks.reshape(512, 512), "c_sel": sel,
        "c_ident": np.eye(128, dtype=md.bfloat16),
    }


def build_gqa_program():
    nc = bass.Bass("TRN2", target_bir_lowering=False, debug=False,
                   detect_race_conditions=False)
    x_in = nc.dram_tensor("x_in", [D, TOK], F32, kind="ExternalInput").ap()
    W = {
        "wq": nc.dram_tensor("wq", [D, 1024], F32, kind="ExternalInput").ap(),
        "wkv": nc.dram_tensor("wkv", [D, 512], F32, kind="ExternalInput").ap(),
        "wo": nc.dram_tensor("wo", [D, D], F32, kind="ExternalInput").ap(),
        "cosq": nc.dram_tensor("cosq", [128, SEQ], F32, kind="ExternalInput").ap(),
        "sinq": nc.dram_tensor("sinq", [128, SEQ], F32, kind="ExternalInput").ap(),
        "costm": nc.dram_tensor("costm", [SEQ, 128], F32, kind="ExternalInput").ap(),
        "sintm": nc.dram_tensor("sintm", [SEQ, 128], F32, kind="ExternalInput").ap(),
        "c_rot": nc.dram_tensor("c_rot", [128, 128], BF16, kind="ExternalInput").ap(),
        "c_mask4": nc.dram_tensor("c_mask4", [512, 512], BF16, kind="ExternalInput").ap(),
        "c_sel": nc.dram_tensor("c_sel", [128, 2], F32, kind="ExternalInput").ap(),
        "c_ident": nc.dram_tensor("c_ident", [128, 128], BF16, kind="ExternalInput").ap(),
    }
    x_out = nc.dram_tensor("x_out", [D, TOK], F32, kind="ExternalOutput").ap()
    ag = {
        "x_my": nc.dram_tensor("ag_xmy", [D, TOK], BF16, kind="Internal").ap(),
        "x_all": nc.dram_tensor("ag_xall", [2 * D, TOK], BF16, kind="Internal").ap(),
        "a_my": nc.dram_tensor("ag_amy", [1024, SEQ], BF16, kind="Internal").ap(),
        "a_all": nc.dram_tensor("ag_aall", [2048, SEQ], BF16, kind="Internal").ap(),
    }
    with contextlib.ExitStack() as ctx:
        C = Ctx(nc, ctx)
        a = gqa_tiles(nc, ctx, C)
        block = ctx.enter_context(nc.Block())
        emit_consts(block, C)
        xh_marker = emit_ln(block, C, x_in)
        emit_gqa(block, C, x_in, x_out, W, a, xh_marker, ag)
        emit_final_wait(block, C)
    return nc



LAMBDA_INIT = 0.8 - 0.6 * float(np.exp(-0.3 * 12))


def emit_dattn(block, C, x_dram, xout_dram, W, a, xh_marker, ag):
    (lx, lw, lw2, ls1, lxr, mm, mm2, mm3, ev, ev2,
     g1, g3, g4, st, st2, cc, xh, aux) = (
        C.s[n] for n in ("lx", "lw", "lw2", "ls1", "lxr", "mm", "mm2", "mm3",
                         "ev", "ev2", "g1", "g3", "g4", "st", "st2", "cc",
                         "xh", "aux"))
    b = {n: C.s[n].v for n in C.SEMS}

    @block.gpsimd
    def _(e):
        e = EngWrap(e, C.chain["gpsimd"])
        e.wait_ge(C.s["g2"].h, xh_marker + 4)
        for k in range(KC):
            e.dma_start(out=ag["x_my"][k * 128:(k + 1) * 128, :],
                        in_=C.XH[:, k, :]).then_inc(g3.h, 16)
        e.wait_ge(g3.h, b["g3"] + 16 * KC)
        e.collective_compute(
            "AllGather", ALU.bypass,
            replica_groups=[[0, 1], [2, 3], [4, 5], [6, 7]],
            ins=[ag["x_my"]], outs=[ag["x_all"]],
        ).then_inc(cc.h, 1)
        e.dma_start(out=a["msk"][:], in_=W["c_mask4"].rearrange(
            "(mi p) f -> p mi f", p=128)).then_inc(lw.h, 16)
        e.dma_start(out=a["sel"][:], in_=W["c_sel"]).then_inc(lw.h, 16)
        e.dma_start(out=a["identb"][:], in_=W["c_ident"]).then_inc(lw.h, 16)
        for i, nm in enumerate(("lq1", "lk1", "lq2", "lk2")):
            e.dma_start(out=a["lvec"][:, i:i + 1], in_=W[nm]).then_inc(lw.h, 16)
        e.dma_start(out=a["sb2"][:], in_=W["subln"]).then_inc(lw.h, 16)
    g3.v += 16 * KC
    cc.v += 1
    cc_xhat = cc.v
    lw.v += 16 * 8
    lw_static = lw.v

    @block.vector
    def _(e):
        e = EngWrap(e, C.chain["vector"])
        e.wait_ge(lw.h, lw_static)
        e.tensor_tensor(a["lam"][:, 0:1], a["lvec"][:, 0:1], a["lvec"][:, 1:2],
                        op=ALU.mult)
        e.tensor_tensor(a["lam"][:, 1:2], a["lvec"][:, 2:3], a["lvec"][:, 3:4],
                        op=ALU.mult)
        e.tensor_copy(a["lamb"][:], a["lam"][:, 0:2]).then_inc(g1.h, 1)
        e.tensor_scalar_mul(a["sb2"][:], a["sb2"][:], 1.0 - LAMBDA_INIT
                            ).then_inc(g1.h, 1)
    g1.v += 2

    @block.tensor
    def _(e):
        e.wait_ge(g1.h, b["g1"] + 1)
        e.wait_ge(C.s["aux"].h, 1)
        ins = e.matmul(C.pb[6][:, 0:2], C.ones_bf[:], a["lamb"][:],
                       start=True, stop=True, skip_group_check=True)
        ins.then_inc(mm3.h, 1)
    mm3.v += 1

    @block.scalar
    def _(e):
        e = EngWrap(e, C.chain["scalar"])
        e.wait_ge(mm3.h, b["mm3"] + 1)
        e.activation(a["lam"][:, 0:2], C.pb[6][:, 0:2], AF.Exp).then_inc(ev2.h, 1)
    ev2.v += 1

    @block.vector
    def _(e):
        e = EngWrap(e, C.chain["vector"])
        e.wait_ge(ev2.h, b["ev2"] + 1)
        e.tensor_tensor(a["lam"][:, 2:3], a["lam"][:, 0:1], a["lam"][:, 1:2],
                        op=ALU.subtract)
        e.tensor_scalar_add(a["lam"][:, 2:3], a["lam"][:, 2:3], LAMBDA_INIT)
        e.tensor_scalar_mul(a["lam"][:, 3:4], a["lam"][:, 2:3], -1.0
                            ).then_inc(g1.h, 1)
    g1.v += 1
    lam_done = g1.v

    ntile = [0]
    for h in range(4):
        for th in range(2):
            bt = {n: C.s[n].v for n in C.SEMS}

            @block.sync
            def _(e):
                e.wait_ge(cc.h, cc_xhat)
                if not (h == 0 and th == 0):
                    e.wait_ge(mm.h, bt["mm"])
                for k in range(KC):
                    e.dma_start(
                        out=C.XH[:, k, :],
                        in_=ag["x_all"][th * 2048 + k * 128:
                                        th * 2048 + (k + 1) * 128, :],
                    ).then_inc(ls1.h, 16)
            ls1.v += 16 * KC

            @block.gpsimd
            def _(e):
                e = EngWrap(e, C.chain["gpsimd"])
                if not (h == 0 and th == 0):
                    e.wait_ge(mm.h, bt["mm"])
                for k in range(KC):
                    e.dma_start(out=a["wkv_s2"][:, k, :],
                                in_=W["wkv"][k * 128:(k + 1) * 128,
                                             h * 512:(h + 1) * 512]
                                ).then_inc(lw.h, 16)
                for k in range(KC):
                    e.dma_start(out=a["wqs2"][:, k, :],
                                in_=W["wq"][k * 128:(k + 1) * 128,
                                            h * 256:(h + 1) * 256]
                                ).then_inc(lw.h, 16)
            lw.v += 16 * 2 * KC

            @block.tensor
            def _(e):
                def cu(*names):
                    for n in names:
                        e.wait_ge(C.s[n].h, bt[n])
                cu("ev", "ev2", "g1", "xh")
                e.wait_ge(ls1.h, bt["ls1"] + 16 * KC)
                e.wait_ge(lw.h, bt["lw"] + 16 * 2 * KC)
                for t in range(KC + 1):
                    if t < KC:
                        pkv = C.pb[t % 2]
                        if t >= 2:
                            e.wait_ge(ev.h, bt["ev"] + 2 * (t - 1))
                        for k in range(KC):
                            ins = e.matmul(pkv[:], C.XH[:, k, t * 128:(t + 1) * 128],
                                           a["wkv_s2"][:, k, :],
                                           start=(k == 0), stop=(k == KC - 1),
                                           skip_group_check=True)
                        ins.then_inc(mm.h, 1)
                    if t >= 1:
                        for kv in range(2):
                            j = 2 * (t - 1) + kv
                            e.wait_ge(ev.h, bt["ev"] + 2 * t)
                            if j >= 1:
                                e.wait_ge(ev2.h, bt["ev2"] + j)
                            ins = e.transpose(
                                C.pb[4].ap().bitcast(BF16)[0:128, 0:128],
                                a["kvt"][:, (t - 1) % 2, kv * 128:(kv + 1) * 128],
                                a["identb"][:])
                            ins.then_inc(mm2.h, 1)
            mm.v += KC
            mm2.v += 2 * KC

            @block.scalar
            def _(e):
                e = EngWrap(e, C.chain["scalar"])
                pb4b2 = C.pb[4].ap().bitcast(BF16)
                for t in range(KC + 1):
                    if t < KC:
                        gt = th * KC + t
                        e.wait_ge(mm.h, bt["mm"] + t + 1)
                        if t >= 2:
                            e.wait_ge(mm2.h, bt["mm2"] + 2 * (t - 1))
                        e.copy(a["kvt"][:, t % 2, :], C.pb[t % 2][:, 0:256]
                               ).then_inc(ev.h, 1)
                        e.copy(a["Vs"][:, gt, :], C.pb[t % 2][:, 256:512]
                               ).then_inc(ev.h, 1)
                    if t >= 1:
                        for kv in range(2):
                            j = 2 * (t - 1) + kv
                            e.wait_ge(mm2.h, bt["mm2"] + j + 1)
                            e.copy(a["Kf"][:, kv,
                                           (th * KC + t - 1) * 128:(th * KC + t) * 128],
                                   pb4b2[0:128, 0:128]).then_inc(ev2.h, 1)
            ev.v += 2 * KC
            ev2.v += 2 * KC

            @block.tensor
            def _(e):
                for u in range(8):
                    qc, tt = u // 4, u % 4
                    pq = C.pb[5] if u % 2 == 0 else C.pb[6]
                    if u >= 2:
                        e.wait_ge(ev.h, bt["ev"] + 2 * KC + u - 1)
                    for k in range(KC):
                        ins = e.matmul(pq[:], a["wqs2"][:, k, qc * 128:(qc + 1) * 128],
                                       C.XH[:, k, tt * 512:(tt + 1) * 512],
                                       start=(k == 0), stop=(k == KC - 1),
                                       skip_group_check=True)
                    ins.then_inc(mm.h, 1)
            mm.v += 8

            @block.scalar
            def _(e):
                e = EngWrap(e, C.chain["scalar"])
                for u in range(8):
                    qc, tt = u // 4, u % 4
                    e.wait_ge(mm.h, bt["mm"] + KC + u + 1)
                    e.copy(a["qsb2"][:, qc, tt * 512:(tt + 1) * 512],
                           C.pb[5][:] if u % 2 == 0 else C.pb[6][:]
                           ).then_inc(ev.h, 1)
            ev.v += 8

            for br in range(2):
                bn = {n: C.s[n].v for n in C.SEMS}

                @block.scalar
                def _(e):
                    e = EngWrap(e, C.chain["scalar"])
                    e.wait_ge(ev.h, bt["ev"] + 2 * KC + 8)
                    if br == 1:
                        e.wait_ge(g1.h, bn["g1"])
                    e.activation(a["sqk"][:], a["qsb2"][:, br, :], AF.Square
                                 ).then_inc(ev2.h, 1)
                    e.wait_ge(g1.h, bn["g1"] + 1)
                    e.activation(a["sqk"][:],
                                 a["Kf"][:, br, th * 2048:(th + 1) * 2048],
                                 AF.Square).then_inc(ev2.h, 1)
                ev2.v += 2

                @block.tensor
                def _(e):
                    e.wait_ge(ev2.h, bn["ev2"] + 1)
                    for tt in range(4):
                        ins = e.matmul(C.pb[5][:], C.ones_bf[:],
                                       a["sqk"][:, tt * 512:(tt + 1) * 512],
                                       start=(tt == 0), stop=(tt == 3),
                                       skip_group_check=True)
                    ins.then_inc(mm3.h, 1)
                    e.wait_ge(ev2.h, bn["ev2"] + 2)
                    e.wait_ge(g1.h, bn["g1"] + 1)
                    for tt in range(4):
                        ins = e.matmul(C.pb[5][:], C.ones_bf[:],
                                       a["sqk"][:, tt * 512:(tt + 1) * 512],
                                       start=(tt == 0), stop=(tt == 3),
                                       skip_group_check=True)
                    ins.then_inc(mm3.h, 1)
                mm3.v += 2

                @block.vector
                def _(e):
                    e = EngWrap(e, C.chain["vector"])
                    e.wait_ge(mm3.h, bn["mm3"] + 1)
                    e.reduce_max(a["n2b"][:], C.pb[5][:], axis=AX.X
                                 ).then_inc(g1.h, 1)
                    e.wait_ge(mm3.h, bn["mm3"] + 2)
                    e.reduce_max(a["n2c"][:], C.pb[5][:], axis=AX.X)
                    if th == 1:
                        e.tensor_tensor(a["nk2"][:, br:br + 1],
                                        a["nk2"][:, br:br + 1], a["n2c"][:],
                                        op=ALU.max)
                    else:
                        e.tensor_copy(a["nk2"][:, br:br + 1], a["n2c"][:])
                    ins = e.tensor_tensor(a["n2b"][:], a["n2b"][:],
                                          a["nk2"][:, br:br + 1], op=ALU.mult)
                    ins.then_inc(g1.h, 1)
                g1.v += 2

                @block.scalar
                def _(e):
                    e = EngWrap(e, C.chain["scalar"])
                    e.wait_ge(g1.h, bn["g1"] + 2)
                    e.activation(a["nC"][:, br:br + 1], a["n2b"][:], AF.Sqrt)
                    e.mul(a["nC"][:, br:br + 1], a["nC"][:, br:br + 1],
                          -(128.0 ** -0.5)).then_inc(ev2.h, 1)
                ev2.v += 1
            negC_done = C.s["ev2"].v

            for qt in range(4):
                bf = {n: C.s[n].v for n in C.SEMS}
                ktn = th * KC + (qt + 1) * 4
                qt0 = th * 2048 + qt * 512
                nt = ntile[0]

                for br in range(2):
                    po = [C.pb[2], C.pb[3]] if br == 0 else [C.pb[5], C.pb[6]]
                    psum_sum = C.pb[4] if br == 0 else C.pb[7]
                    bb = {n: C.s[n].v for n in C.SEMS}

                    @block.tensor
                    def _(e):
                        e.wait_ge(ev2.h, negC_done)
                        if br == 0:
                            e.wait_ge(xh.h, bf["xh"])
                        for kt in range(ktn + 1):
                            if kt < ktn:
                                ps = C.pb[kt % 2]
                                if kt >= 2:
                                    e.wait_ge(ev.h, bb["ev"] + kt - 1)
                                ins = e.matmul(ps[:],
                                               a["Kf"][:, br, kt * 128:(kt + 1) * 128],
                                               a["qsb2"][:, br,
                                                         qt * 512:(qt + 1) * 512],
                                               start=True, stop=True,
                                               skip_group_check=True)
                                ins.then_inc(mm.h, 1)
                            if kt >= 1:
                                pk = kt - 1
                                diag = (pk * 128 >= qt0)
                                if diag:
                                    e.wait_ge(g1.h, bb["g1"] + (pk - (ktn - 4)) + 1)
                                else:
                                    e.wait_ge(ev.h, bb["ev"] + pk + 1)
                                for dv in range(2):
                                    e.matmul(po[dv][:],
                                             a["Vs"][:, pk, dv * 128:(dv + 1) * 128],
                                             a["ptile"][:, pk % 2, :],
                                             start=(pk == 0), stop=(pk == ktn - 1),
                                             skip_group_check=True)
                                ins2 = e.matmul(psum_sum[:], C.ones_bf[:],
                                                a["ptile"][:, pk % 2, :],
                                                start=(pk == 0), stop=(pk == ktn - 1),
                                                skip_group_check=True)
                                if pk == ktn - 1:
                                    ins2.then_inc(mm2.h, 1)
                    mm.v += ktn
                    mm2.v += 1

                    @block.scalar
                    def _(e):
                        e = EngWrap(e, C.chain["scalar"])
                        for kt in range(ktn):
                            e.wait_ge(mm.h, bb["mm"] + kt + 1)
                            e.activation(a["ptile"][:, kt % 2, :], C.pb[kt % 2][:],
                                         AF.Exp, bias=a["nC"][:, br:br + 1],
                                         scale=128.0 ** -0.5).then_inc(ev.h, 1)
                    ev.v += ktn

                    @block.vector
                    def _(e):
                        e = EngWrap(e, C.chain["vector"])
                        for kt in range(ktn - 4, ktn):
                            di = kt - (ktn - 4)
                            e.wait_ge(ev.h, bb["ev"] + kt + 1)
                            e.tensor_tensor(a["ptile"][:, kt % 2, :],
                                            a["ptile"][:, kt % 2, :],
                                            a["msk"][:, di, :],
                                            op=ALU.mult).then_inc(g1.h, 1)
                    g1.v += 4

                @block.vector
                def _(e):
                    e = EngWrap(e, C.chain["vector"])
                    e.wait_ge(mm2.h, bf["mm2"] + 2)
                    e.wait_ge(g1.h, lam_done)
                    if nt >= 1:
                        e.wait_ge(g4.h, b["g4"] + 16 * 2 * nt)
                    e.reciprocal(a["rs"][:], C.pb[4][:])
                    e.tensor_tensor(a["o1c"][:, 0, :], C.pb[2][:], a["rs"][:],
                                    op=ALU.mult)
                    e.tensor_tensor(a["o1c"][:, 1, :], C.pb[3][:], a["rs"][:],
                                    op=ALU.mult)
                    e.reciprocal(a["rs"][:], C.pb[7][:])
                    e.tensor_tensor(a["acm"][:, 0, :], C.pb[5][:], a["rs"][:],
                                    op=ALU.mult)
                    e.tensor_tensor(a["acm"][:, 1, :], C.pb[6][:], a["rs"][:],
                                    op=ALU.mult)
                    for ch2 in range(2):
                        ins = e.scalar_tensor_tensor(
                            a["acm"][:, ch2, :], a["acm"][:, ch2, :],
                            a["lam"][:, 3:4], a["o1c"][:, ch2, :],
                            op0=ALU.mult, op1=ALU.add)
                    ins.then_inc(xh.h, 1)
                xh.v += 1

                @block.scalar
                def _(e):
                    e = EngWrap(e, C.chain["scalar"])
                    e.wait_ge(xh.h, bf["xh"] + 1)
                    e.activation(a["ptile"][:, 0, :], a["acm"][:, 0, :], AF.Square)
                    e.activation(a["ptile"][:, 1, :], a["acm"][:, 1, :], AF.Square
                                 ).then_inc(ev2.h, 1)
                ev2.v += 1

                @block.tensor
                def _(e):
                    e.wait_ge(ev2.h, bf["ev2"] + 1)
                    for ch2 in range(2):
                        ins = e.matmul(C.pb[2][:], C.ones_bf[:],
                                       a["ptile"][:, ch2, :],
                                       start=(ch2 == 0), stop=(ch2 == 1),
                                       skip_group_check=True)
                    ins.then_inc(mm3.h, 1)
                mm3.v += 1

                @block.vector
                def _(e):
                    e = EngWrap(e, C.chain["vector"])
                    e.wait_ge(mm3.h, bf["mm3"] + 1)
                    e.tensor_scalar_mul(a["rs"][:], C.pb[2][:], 1.0 / 256.0)
                    e.tensor_scalar_add(a["rs"][:], a["rs"][:], 1e-5)
                    e.reciprocal(a["rs"][:], a["rs"][:]).then_inc(g1.h, 1)
                g1.v += 1

                @block.scalar
                def _(e):
                    e = EngWrap(e, C.chain["scalar"])
                    e.wait_ge(g1.h, bf["g1"] + 8 + 1)
                    e.activation(a["rs"][:], a["rs"][:], AF.Sqrt).then_inc(ev2.h, 1)
                ev2.v += 1

                @block.vector
                def _(e):
                    e = EngWrap(e, C.chain["vector"])
                    e.wait_ge(ev2.h, bf["ev2"] + 2)
                    for ch2 in range(2):
                        e.tensor_tensor(a["at2"][:, nt % 2, ch2, :],
                                        a["acm"][:, ch2, :], a["rs"][:],
                                        op=ALU.mult)
                        ins = e.tensor_scalar_mul(a["at2"][:, nt % 2, ch2, :],
                                                  a["at2"][:, nt % 2, ch2, :],
                                                  a["sb2"][:, ch2:ch2 + 1])
                    ins.then_inc(xh.h, 1)
                xh.v += 1

                @block.gpsimd
                def _(e):
                    e = EngWrap(e, C.chain["gpsimd"])
                    e.wait_ge(xh.h, bf["xh"] + 2)
                    for ch2 in range(2):
                        e.dma_start(out=ag["a_my"][h * 256 + ch2 * 128:
                                                   h * 256 + (ch2 + 1) * 128,
                                                   qt0:qt0 + 512],
                                    in_=a["at2"][:, nt % 2, ch2, :]
                                    ).then_inc(g4.h, 16)
                g4.v += 32
                ntile[0] += 1

    @block.gpsimd
    def _(e):
        e = EngWrap(e, C.chain["gpsimd"])
        e.wait_ge(g4.h, b["g4"] + 16 * 64)
        e.collective_compute(
            "AllGather", ALU.bypass,
            replica_groups=[[0, 1], [2, 3], [4, 5], [6, 7]],
            ins=[ag["a_my"]], outs=[ag["a_all"]],
        ).then_inc(cc.h, 1)
    cc.v += 1
    emit_outproj(block, C, x_dram, xout_dram, W["wo"], a, ag, cc.v)


def dattn_tiles(nc, ctx, C):
    a = {}
    def sb(name, shape, dt):
        a[name] = ctx.enter_context(nc.sbuf_tensor("d_" + name, shape, dt)).ap()
    sb("msk", [128, 4, 512], BF16)
    sb("sel", [128, 2], F32)
    sb("identb", [128, 128], BF16)
    sb("lvec", [128, 4], F32)
    sb("lam", [128, 4], F32)
    sb("lamb", [128, 2], BF16)
    sb("sb2", [128, 2], F32)
    sb("wkv_s2", [128, KC, 512], BF16)
    sb("wqs2", [128, KC, 256], BF16)
    sb("kvt", [128, 2, 256], BF16)
    sb("Kf", [128, 2, 4096], BF16)
    sb("Vs", [128, 32, 256], BF16)
    sb("qsb2", [128, 2, 2048], BF16)
    sb("sqk", [128, 2048], BF16)
    sb("nC", [128, 2], F32)
    sb("n2b", [128, 1], F32)
    sb("n2c", [128, 1], F32)
    sb("nk2", [128, 2], F32)
    sb("ptile", [128, 2, 512], BF16)
    sb("rs", [128, 512], F32)
    sb("o1c", [128, 2, 512], F32)
    sb("acm", [128, 2, 512], F32)
    sb("at2", [128, 2, 2, 512], BF16)
    sb("ats", [128, 4, 2, 512], BF16)
    sb("atc", [128, KC, 512], BF16)
    sb("yt", [128, 2, 512], F32)
    sb("xr", [128, 2, 512], F32)
    a["wos"] = C.XH.ap()
    return a


def dattn_host_inputs(inp, core):
    import ml_dtypes as md
    g = core % 2
    wq = np.asarray(inp["L0_attn_b_Wq"], np.float32)
    wk = np.asarray(inp["L0_attn_b_Wk"], np.float32)
    wv = np.asarray(inp["L0_attn_b_Wv"], np.float32)
    wo = np.asarray(inp["L0_attn_b_Wo"], np.float32)
    wq_my = np.ascontiguousarray(wq[:, g * 1024:(g + 1) * 1024])
    blocks = []
    for lh in range(4):
        gh = g * 4 + lh
        blocks.append(wk[:, gh * 256:(gh + 1) * 256])
        blocks.append(wv[:, gh * 256:(gh + 1) * 256])
    wkv_my = np.ascontiguousarray(np.concatenate(blocks, axis=1))
    masks = np.zeros((4, 128, 512), dtype=md.bfloat16)
    for di in range(4):
        f = np.arange(512)[None, :]
        p = np.arange(128)[:, None]
        masks[di] = (f >= p + 128 * di).astype(md.bfloat16)
    sel = np.zeros((128, 2), np.float32)
    sel[:, g] = 1.0
    subln = np.ascontiguousarray(
        np.asarray(inp["L0_attn_b_subln_weight"], np.float32).reshape(2, 128).T)
    return {
        "wq": wq_my, "wkv": wkv_my, "wo": wo,
        "c_mask4": masks.reshape(512, 512), "c_sel": sel,
        "c_ident": np.eye(128, dtype=md.bfloat16),
        "lq1": np.asarray(inp["L0_attn_b_lambda_q1"], np.float32).reshape(128, 1),
        "lk1": np.asarray(inp["L0_attn_b_lambda_k1"], np.float32).reshape(128, 1),
        "lq2": np.asarray(inp["L0_attn_b_lambda_q2"], np.float32).reshape(128, 1),
        "lk2": np.asarray(inp["L0_attn_b_lambda_k2"], np.float32).reshape(128, 1),
        "subln": subln,
    }


def build_dattn_program():
    nc = bass.Bass("TRN2", target_bir_lowering=False, debug=False,
                   detect_race_conditions=False)
    x_in = nc.dram_tensor("x_in", [D, TOK], F32, kind="ExternalInput").ap()
    W = {}
    for nm, shp, dt in (("wq", [D, 1024], F32), ("wkv", [D, 2048], F32),
                        ("wo", [D, D], F32), ("c_mask4", [512, 512], BF16),
                        ("c_sel", [128, 2], F32), ("c_ident", [128, 128], BF16),
                        ("lq1", [128, 1], F32), ("lk1", [128, 1], F32),
                        ("lq2", [128, 1], F32), ("lk2", [128, 1], F32),
                        ("subln", [128, 2], F32)):
        W[nm] = nc.dram_tensor(nm, shp, dt, kind="ExternalInput").ap()
    x_out = nc.dram_tensor("x_out", [D, TOK], F32, kind="ExternalOutput").ap()
    ag = {
        "x_my": nc.dram_tensor("ag_xmy", [D, TOK], BF16, kind="Internal").ap(),
        "x_all": nc.dram_tensor("ag_xall", [2 * D, TOK], BF16, kind="Internal").ap(),
        "a_my": nc.dram_tensor("ag_amy", [1024, SEQ], BF16, kind="Internal").ap(),
        "a_all": nc.dram_tensor("ag_aall", [2048, SEQ], BF16, kind="Internal").ap(),
    }
    with contextlib.ExitStack() as ctx:
        C = Ctx(nc, ctx)
        a = dattn_tiles(nc, ctx, C)
        block = ctx.enter_context(nc.Block())
        emit_consts(block, C)
        xh_marker = emit_ln(block, C, x_in)
        emit_dattn(block, C, x_in, x_out, W, a, xh_marker, ag)
        emit_final_wait(block, C)
    return nc



def emit_gla(block, C, x_dram, xout_dram, W, a, xh_marker, ag, od):
    (lx, lw, lw2, ls1, lxr, mm, mm2, mm3, ev, ev2,
     g1, g3, g4, st, st2, cc, xh, aux) = (
        C.s[n] for n in ("lx", "lw", "lw2", "ls1", "lxr", "mm", "mm2", "mm3",
                         "ev", "ev2", "g1", "g3", "g4", "st", "st2", "cc",
                         "xh", "aux"))
    b = {n: C.s[n].v for n in C.SEMS}
    SCALE = 128.0 ** -0.5
    pb4b = C.pb[4].ap().bitcast(BF16)
    ksig = C.mu.ap()[:, 0:1024]
    cg = C.rstd.ap()[:, 0:1024]

    @block.gpsimd
    def _(e):
        e = EngWrap(e, C.chain["gpsimd"])
        e.wait_ge(C.s["g2"].h, xh_marker + 4)
        for k in range(KC):
            e.dma_start(out=ag["x_my"][k * 128:(k + 1) * 128, :],
                        in_=C.XH[:, k, :]).then_inc(g3.h, 16)
        e.wait_ge(g3.h, b["g3"] + 16 * KC)
        e.collective_compute(
            "AllGather", ALU.bypass,
            replica_groups=[[0, 1], [2, 3], [4, 5], [6, 7]],
            ins=[ag["x_my"]], outs=[ag["x_all"]],
        ).then_inc(cc.h, 1)
        for nm in ("gmask", "mA", "mB", "sel", "identb", "gw"):
            e.dma_start(out=a[nm][:], in_=W["c_" + nm]).then_inc(lw.h, 16)
    g3.v += 16 * KC
    cc.v += 1
    cc_xhat = cc.v
    lw.v += 16 * 6
    lw_static = lw.v

    @block.vector
    def _(e):
        e = EngWrap(e, C.chain["vector"])
        e.memset(a["S"][:], 0.0)
        e.memset(a["Sb"][:], 0.0)
        e.memset(a["ss"][:], 0.0).then_inc(g1.h, 1)
    g1.v += 1

    og_j = [0]
    ogb = {"st": st.v, "st2": st2.v}

    for th in range(2):
        b_th = {n: C.s[n].v for n in C.SEMS}

        @block.sync
        def _(e):
            e.wait_ge(cc.h, cc_xhat)
            e.wait_ge(mm.h, b_th["mm"])
            for k in range(KC):
                e.dma_start(
                    out=C.XH[:, k, :],
                    in_=ag["x_all"][th * 2048 + k * 128:
                                    th * 2048 + (k + 1) * 128, :],
                ).then_inc(ls1.h, 16)
        ls1.v += 16 * KC
        xh_ls1 = ls1.v

        for h in range(8):
            bu = {n: C.s[n].v for n in C.SEMS}

            @block.gpsimd
            def _(e):
                e = EngWrap(e, C.chain["gpsimd"])
                e.wait_ge(mm.h, bu["mm"])
                for k in range(KC):
                    e.dma_start(out=a["wqkv"][:, k, :],
                                in_=W["wqkv"][k * 128:(k + 1) * 128,
                                              h * 384:(h + 1) * 384]
                                ).then_inc(lw.h, 16)
            lw.v += 16 * KC
            lw_u = lw.v

            for sh in range(2):
                bs = {n: C.s[n].v for n in C.SEMS}
                t0 = sh * 1024
                col0 = th * 2048 + sh * 1024

                def m_scores(t):
                    return bs["mm"] + 12 + 3 * t + 1

                def m_o(t):
                    return bs["mm"] + 12 + 3 * t + 2

                def m_state(t):
                    return bs["mm"] + 12 + 3 * t + 3

                def g1P(t):
                    return bs["g1"] + 4 + 4 * t + 1 + (1 if t >= 4 else 0)

                def g1q2(t):
                    return g1P(t) + 1

                def g1k2(t):
                    return g1P(t) + 2

                def g1Sb(t):
                    return g1P(t) + 3
                g1_gate = bs["g1"] + 4
                g1_ssadd1 = bs["g1"] + 21
                g1_ssadd2 = bs["g1"] + 38

                @block.tensor
                def _(e):
                    for n in ("ev", "ev2", "g1", "xh"):
                        e.wait_ge(C.s[n].h, bs[n])
                    e.wait_ge(ls1.h, xh_ls1)
                    e.wait_ge(lw.h, lw_u)
                    for tt in range(2):
                        if tt >= 1:
                            e.wait_ge(ev.h, bs["ev"] + 2 * tt)
                        sl = slice(t0 + tt * 512, t0 + (tt + 1) * 512)
                        for k in range(KC):
                            ins = e.matmul(C.pb[0][:], a["wqkv"][:, k, 0:128],
                                           C.XH[:, k, sl],
                                           start=(k == 0), stop=(k == KC - 1),
                                           skip_group_check=True)
                        ins.then_inc(mm.h, 1)
                        for k in range(KC):
                            ins = e.matmul(C.pb[1][:], a["wqkv"][:, k, 128:256],
                                           C.XH[:, k, sl],
                                           start=(k == 0), stop=(k == KC - 1),
                                           skip_group_check=True)
                        ins.then_inc(mm.h, 1)
                    for t in range(8):
                        if t >= 2:
                            e.wait_ge(ev.h, bs["ev"] + 4 + t - 1)
                        sl = slice(t0 + t * 128, t0 + (t + 1) * 128)
                        for k in range(KC):
                            ins = e.matmul(C.pb[2 + t % 2][:, 0:128],
                                           C.XH[:, k, sl],
                                           a["wqkv"][:, k, 256:384],
                                           start=(k == 0), stop=(k == KC - 1),
                                           skip_group_check=True)
                        ins.then_inc(mm.h, 1)
                mm.v += 12

                @block.scalar
                def _(e):
                    e = EngWrap(e, C.chain["scalar"])
                    for tt in range(2):
                        sl = slice(tt * 512, (tt + 1) * 512)
                        e.wait_ge(mm.h, bs["mm"] + 2 * tt + 1)
                        e.activation(a["qrs"][:, sl], C.pb[0][:], AF.Relu,
                                     scale=SCALE).then_inc(ev.h, 1)
                        e.wait_ge(mm.h, bs["mm"] + 2 * tt + 2)
                        e.copy(a["kraw"][:, sl], C.pb[1][:]).then_inc(ev.h, 1)
                    for t in range(8):
                        e.wait_ge(mm.h, bs["mm"] + 4 + t + 1)
                        e.copy(a["Vt"][:, t, :], C.pb[2 + t % 2][:, 0:128]
                               ).then_inc(ev.h, 1)
                ev.v += 12

                @block.scalar
                def _(e):
                    e = EngWrap(e, C.chain["scalar"])
                    e.activation(ksig, a["kraw"][:], AF.Sigmoid).then_inc(ev2.h, 1)
                    e.activation(a["tmp"][:], a["kraw"][:], AF.Abs
                                 ).then_inc(ev2.h, 1)
                    e.activation(a["tmp"][:], a["tmp"][:], AF.Exp, scale=-1.0)
                    e.activation(a["tmp"][:], a["tmp"][:], AF.Ln, bias=1.0
                                 ).then_inc(ev2.h, 1)
                    e.wait_ge(g1.h, bs["g1"] + 1)
                    e.activation(a["fac"][:], a["tmp"][:], AF.Exp
                                 ).then_inc(ev2.h, 1)
                    e.wait_ge(g1.h, bs["g1"] + 2)
                    e.activation(a["fac"][:], a["tmp"][:], AF.Exp, scale=-1.0
                                 ).then_inc(ev2.h, 1)
                    e.activation(a["E1"][:], cg, AF.Exp).then_inc(ev2.h, 1)
                ev2.v += 6

                @block.vector
                def _(e):
                    e = EngWrap(e, C.chain["vector"])
                    e.wait_ge(ev2.h, bs["ev2"] + 2)
                    e.tensor_scalar_max(a["kraw"][:], a["kraw"][:], 0.0)
                    e.wait_ge(ev2.h, bs["ev2"] + 3)
                    e.tensor_tensor(a["kraw"][:], a["kraw"][:], a["tmp"][:],
                                    op=ALU.add)
                    e.tensor_scalar_mul(a["kraw"][:], a["kraw"][:], -1.0)
                    e.wait_ge(lw.h, lw_static)
                    e.tensor_tensor_scan(cg, a["gmask"][:], a["kraw"][:],
                                         0.0, op0=ALU.mult, op1=ALU.add)
                    e.tensor_copy(a["cgc"][:], cg[:, 63:1024:64])
                    e.tensor_scalar_mul(a["cgc"][:], a["cgc"][:], -1.0)
                    cgs = cg.rearrange("p (c s) -> p s c", s=64)
                    tms = a["tmp"].rearrange("p (c s) -> p s c", s=64)
                    ncB = a["cgc"].rearrange("p (o c) -> p o c", o=1
                                             ).broadcast_to((128, 64, 16))
                    e.tensor_tensor(tms, cgs, ncB, op=ALU.add).then_inc(g1.h, 1)
                    e.wait_ge(ev2.h, bs["ev2"] + 4)
                    e.tensor_tensor(a["qend"][:], a["qrs"][:], a["fac"][:],
                                    op=ALU.mult).then_inc(g1.h, 1)
                    e.wait_ge(ev2.h, bs["ev2"] + 5)
                    e.tensor_tensor(a["Khat"][:], ksig, a["fac"][:],
                                    op=ALU.mult).then_inc(g1.h, 1)
                    e.wait_ge(ev2.h, bs["ev2"] + 6)
                    e.tensor_tensor(a["qhat"][:], a["qrs"][:], a["E1"][:],
                                    op=ALU.mult).then_inc(g1.h, 1)
                g1.v += 4

                @block.tensor
                def _(e):
                    e.wait_ge(g1.h, g1_gate)
                    e.wait_ge(ev.h, bs["ev"] + 4)
                    for t in range(8):
                        c = slice(t * 128, (t + 1) * 128)
                        if t >= 1:
                            e.wait_ge(g1.h, g1P(t - 1))
                        e.matmul(C.pb[0][:, 0:128], a["Khat"][:, c],
                                 a["qend"][:, c], start=True, stop=True,
                                 skip_group_check=True)
                        ins = e.matmul(C.pb[1][:, 0:128], a["Khat"][:, c],
                                       a["qhat"][:, c], start=True, stop=True,
                                       skip_group_check=True)
                        ins.then_inc(mm.h, 1)
                        if t < 2:
                            e.wait_ge(ev.h, bs["ev"] + 12)
                        else:
                            e.wait_ge(xh.h, bs["xh"] + t - 1)
                        e.wait_ge(g1.h, g1P(t))
                        e.matmul(C.pb[2 + t % 2][:, 0:128], a["Vt"][:, t, :],
                                 a["pt"][:, t % 2, :], start=True, stop=False,
                                 skip_group_check=True)
                        e.wait_ge(g1.h, g1q2(t))
                        ins = e.matmul(C.pb[2 + t % 2][:, 0:128],
                                       a["Sb"][:, h, :], a["q2"][:, t % 2, :],
                                       start=False, stop=True,
                                       skip_group_check=True)
                        ins.then_inc(mm.h, 1)
                        e.wait_ge(g1.h, g1k2(t))
                        if t >= 1:
                            e.wait_ge(ev.h, bs["ev"] + 12 + t)
                        ins = e.transpose(pb4b[0:128, 0:128],
                                          a["k2"][:, t % 2, :], a["identb"][:])
                        ins.then_inc(mm2.h, 1)
                        e.wait_ge(ev.h, bs["ev"] + 12 + t + 1)
                        ins = e.matmul(C.pb[5][:, 0:128], a["k2tm"][:, t % 2, :],
                                       a["Vt"][:, t, :], start=True, stop=True,
                                       skip_group_check=True)
                        ins.then_inc(mm.h, 1)
                        e.wait_ge(ev2.h, bs["ev2"] + 6 + t + 1)
                        if t >= 4:
                            e.wait_ge(g1.h, g1_ssadd1)
                        ins = e.matmul(C.pb[6][:, (t % 4) * 128:(t % 4 + 1) * 128],
                                       C.ones_bf[:], a["sqt"][:, t % 2, :],
                                       start=True, stop=True,
                                       skip_group_check=True)
                        ins.then_inc(mm3.h, 1)
                mm.v += 24
                mm2.v += 8
                mm3.v += 8

                @block.scalar
                def _(e):
                    e = EngWrap(e, C.chain["scalar"])
                    for t in range(8):
                        e.wait_ge(mm2.h, bs["mm2"] + t + 1)
                        e.copy(a["k2tm"][:, t % 2, :], pb4b[0:128, 0:128]
                               ).then_inc(ev.h, 1)
                        j = og_j[0] + t
                        if j >= 2:
                            psem = st if j % 2 == 0 else st2
                            pb0v = ogb["st"] if j % 2 == 0 else ogb["st2"]
                            e.wait_ge(psem.h, pb0v + 16 * (j // 2))
                        e.wait_ge(mm.h, m_o(t))
                        e.copy(a["og2"][:, t % 2, :], C.pb[2 + t % 2][:, 0:128]
                               ).then_inc(xh.h, 1)
                        if t >= 2:
                            e.wait_ge(mm3.h, bs["mm3"] + t - 1)
                        e.activation(a["sqt"][:, t % 2, :], a["og2"][:, t % 2, :],
                                     AF.Square).then_inc(ev2.h, 1)
                ev.v += 8
                ev2.v += 8
                xh.v += 8

                @block.vector
                def _(e):
                    e = EngWrap(e, C.chain["vector"])
                    for t in range(8):
                        c0 = slice(t * 128, t * 128 + 64)
                        c1 = slice(t * 128 + 64, (t + 1) * 128)
                        e63 = a["E1"][:, t * 128 + 63:t * 128 + 64]
                        e127 = a["E1"][:, t * 128 + 127:t * 128 + 128]
                        e.wait_ge(mm.h, m_scores(t))
                        e.tensor_tensor(a["pt2"][:], C.pb[1][:, 0:128],
                                        a["mB"][:], op=ALU.mult)
                        e.tensor_tensor(a["pt"][:, t % 2, :], C.pb[0][:, 0:128],
                                        a["mA"][:], op=ALU.mult)
                        e.tensor_tensor(a["pt"][:, t % 2, :], a["pt"][:, t % 2, :],
                                        a["pt2"][:], op=ALU.add).then_inc(g1.h, 1)
                        e.tensor_copy(a["q2"][:, t % 2, 0:64], a["qhat"][:, c0])
                        e.tensor_scalar_mul(a["q2"][:, t % 2, 64:128],
                                            a["qhat"][:, c1], e63
                                            ).then_inc(g1.h, 1)
                        if t >= 2:
                            e.wait_ge(mm2.h, bs["mm2"] + t - 1)
                        e.tensor_scalar_mul(a["k2"][:, t % 2, 0:64],
                                            a["Khat"][:, c0], e127)
                        e.tensor_copy(a["k2"][:, t % 2, 64:128], a["Khat"][:, c1]
                                      ).then_inc(g1.h, 1)
                        e.tensor_tensor(a["dec"][:], e63, e127, op=ALU.mult)
                        e.wait_ge(mm.h, m_state(t))
                        e.scalar_tensor_tensor(a["S"][:, h, :], a["S"][:, h, :],
                                               a["dec"][:], C.pb[5][:, 0:128],
                                               op0=ALU.mult, op1=ALU.add)
                        e.tensor_copy(a["Sb"][:, h, :], a["S"][:, h, :]
                                      ).then_inc(g1.h, 1)
                        if t == 3 or t == 7:
                            grp = t // 4
                            sl = slice(col0 + grp * 512, col0 + (grp + 1) * 512)
                            e.wait_ge(mm3.h, bs["mm3"] + 4 * (grp + 1))
                            e.tensor_tensor(a["ss"][0:1, sl], a["ss"][0:1, sl],
                                            C.pb[6][0:1, 0:512], op=ALU.add
                                            ).then_inc(g1.h, 1)
                g1.v += 34

                @block.sync
                def _(e):
                    for t in range(8):
                        j = og_j[0] + t
                        e.wait_ge(xh.h, bs["xh"] + t + 1)
                        e.dma_start(
                            out=od["o_raw"][h * 128:(h + 1) * 128,
                                            col0 + t * 128:col0 + (t + 1) * 128],
                            in_=a["og2"][:, t % 2, :]
                        ).then_inc((st if j % 2 == 0 else st2).h, 16)
                st.v += 16 * 4
                st2.v += 16 * 4
                og_j[0] += 8

    b2 = {n: C.s[n].v for n in C.SEMS}

    @block.gpsimd
    def _(e):
        e = EngWrap(e, C.chain["gpsimd"])
        e.wait_ge(g1.h, b2["g1"])
        e.dma_start(out=od["ss_my"][:], in_=a["ss"][0:1, :]).then_inc(g3.h, 16)
        e.wait_ge(g3.h, b2["g3"] + 16)
        e.collective_compute(
            "AllGather", ALU.bypass,
            replica_groups=[[0, 1], [2, 3], [4, 5], [6, 7]],
            ins=[od["ss_my"]], outs=[od["ss_all"]],
        ).then_inc(cc.h, 1)
        e.wait_ge(cc.h, b2["cc"] + 1)
        e.dma_start(out=a["ssa"][0:1, 0:4096], in_=od["ss_all"][0:1, :]
                    ).then_inc(lw2.h, 16)
        e.dma_start(out=a["ssa"][0:1, 4096:8192], in_=od["ss_all"][1:2, :]
                    ).then_inc(lw2.h, 16)
    g3.v += 16
    cc.v += 1
    lw2.v += 32

    b3 = {n: C.s[n].v for n in C.SEMS}

    @block.vector
    def _(e):
        e = EngWrap(e, C.chain["vector"])
        e.wait_ge(lw2.h, b3["lw2"])
        e.tensor_tensor(a["ss"][0:1, :], a["ssa"][0:1, 0:4096],
                        a["ssa"][0:1, 4096:8192], op=ALU.add).then_inc(g1.h, 1)
    g1.v += 1
    g1_ssm = g1.v

    @block.tensor
    def _(e):
        e.wait_ge(g1.h, g1_ssm)
        for tt in range(8):
            if tt >= 1:
                e.wait_ge(xh.h, b3["xh"] + 8 * tt)
            ins = e.matmul(C.pb[7][:], C.ones_bf[0:1, :],
                           a["ss"][0:1, tt * 512:(tt + 1) * 512],
                           start=True, stop=True, skip_group_check=True)
            ins.then_inc(mm3.h, 1)
    mm3.v += 8

    @block.scalar
    def _(e):
        e = EngWrap(e, C.chain["scalar"])
        for tt in range(8):
            e.wait_ge(g1.h, g1_ssm + tt + 1)
            e.activation(C.pb[7][:], C.pb[7][:], AF.Sqrt).then_inc(ev2.h, 1)
    ev2.v += 8

    @block.sync
    def _(e):
        e.wait_ge(st.h, ogb["st"] + 16 * 128)
        e.wait_ge(st2.h, ogb["st2"] + 16 * 128)
        j = 0
        for tt in range(8):
            for lh in range(8):
                if j >= 2:
                    e.wait_ge(xh.h, b3["xh"] + j - 1)
                e.dma_start(out=a["otr"][:, j % 2, :],
                            in_=od["o_raw"][lh * 128:(lh + 1) * 128,
                                            tt * 512:(tt + 1) * 512]
                            ).then_inc(ls1.h, 16)
                e.wait_ge(xh.h, b3["xh"] + j + 1)
                e.dma_start(out=ag["a_my"][lh * 128:(lh + 1) * 128,
                                           tt * 512:(tt + 1) * 512],
                            in_=a["at2"][:, j % 2, :]
                            ).then_inc((st if j % 2 == 0 else st2).h, 16)
                j += 1
    ls1.v += 16 * 64
    st.v += 16 * 32
    st2.v += 16 * 32

    @block.vector
    def _(e):
        e = EngWrap(e, C.chain["vector"])
        j = 0
        for tt in range(8):
            e.wait_ge(mm3.h, b3["mm3"] + tt + 1)
            e.tensor_scalar_mul(C.pb[7][:], C.pb[7][:], 1.0 / 2048.0)
            e.tensor_scalar_add(C.pb[7][:], C.pb[7][:], 1e-5)
            e.reciprocal(C.pb[7][:], C.pb[7][:]).then_inc(g1.h, 1)
            for lh in range(8):
                e.wait_ge(ev2.h, b3["ev2"] + tt + 1)
                e.wait_ge(ls1.h, b3["ls1"] + 16 * (j + 1))
                if j >= 2:
                    psem = st if j % 2 == 0 else st2
                    pbv = b3["st"] if j % 2 == 0 else b3["st2"]
                    e.wait_ge(psem.h, pbv + 16 * (j // 2))
                e.tensor_tensor(a["at2"][:, j % 2, :], a["otr"][:, j % 2, :],
                                C.pb[7][:], op=ALU.mult)
                e.tensor_scalar_mul(a["at2"][:, j % 2, :], a["at2"][:, j % 2, :],
                                    a["gw"][:, lh:lh + 1]).then_inc(xh.h, 1)
                j += 1
    g1.v += 8
    xh.v += 64

    @block.gpsimd
    def _(e):
        e = EngWrap(e, C.chain["gpsimd"])
        e.wait_ge(st.h, st.v)
        e.wait_ge(st2.h, st2.v)
        e.collective_compute(
            "AllGather", ALU.bypass,
            replica_groups=[[0, 1], [2, 3], [4, 5], [6, 7]],
            ins=[ag["a_my"]], outs=[ag["a_all"]],
        ).then_inc(cc.h, 1)
    cc.v += 1
    emit_outproj(block, C, x_dram, xout_dram, W["wo"], a, ag, cc.v)


def gla_tiles(nc, ctx, C):
    a = {}
    def sb(name, shape, dt):
        a[name] = ctx.enter_context(nc.sbuf_tensor("u_" + name, shape, dt)).ap()
    sb("wqkv", [128, KC, 384], BF16)
    sb("kraw", [128, 1024], F32)
    sb("tmp", [128, 1024], F32)
    sb("fac", [128, 1024], BF16)
    sb("E1", [128, 1024], F32)
    sb("qrs", [128, 1024], BF16)
    sb("qend", [128, 1024], BF16)
    sb("Khat", [128, 1024], BF16)
    sb("qhat", [128, 1024], BF16)
    sb("Vt", [128, 8, 128], BF16)
    sb("gmask", [128, 1024], F32)
    sb("cgc", [128, 16], F32)
    sb("dec", [128, 1], F32)
    sb("S", [128, 8, 128], F32)
    sb("Sb", [128, 8, 128], BF16)
    sb("pt", [128, 2, 128], BF16)
    sb("pt2", [128, 128], BF16)
    sb("q2", [128, 2, 128], BF16)
    sb("k2", [128, 2, 128], BF16)
    sb("k2tm", [128, 2, 128], BF16)
    sb("og2", [128, 2, 128], BF16)
    sb("sqt", [128, 2, 128], BF16)
    sb("ss", [1, 4096], BF16)
    sb("ssa", [1, 8192], BF16)
    sb("mA", [128, 128], BF16)
    sb("mB", [128, 128], BF16)
    sb("gw", [128, 8], F32)
    sb("sel", [128, 2], F32)
    sb("identb", [128, 128], BF16)
    sb("otr", [128, 2, 512], BF16)
    sb("at2", [128, 2, 512], BF16)
    sb("ats", [128, 4, 2, 512], BF16)
    sb("atc", [128, KC, 512], BF16)
    sb("yt", [128, 2, 512], F32)
    sb("xr", [128, 2, 512], F32)
    a["wos"] = C.XH.ap()
    return a


def gla_host_inputs(inp, core):
    import ml_dtypes as md
    g = core % 2
    wq = np.asarray(inp["L1_attn_b_Wq"], np.float32)
    wk = np.asarray(inp["L1_attn_b_Wk"], np.float32)
    wv = np.asarray(inp["L1_attn_b_Wi"], np.float32)
    wo = np.asarray(inp["L1_attn_b_Wo"], np.float32)
    gwv = np.asarray(inp["L1_attn_b_g_weight"], np.float32).reshape(-1)
    blocks = []
    for lh in range(8):
        gh = g * 8 + lh
        blocks.append(wq[:, gh * 128:(gh + 1) * 128])
        blocks.append(wk[:, gh * 128:(gh + 1) * 128])
        blocks.append(wv[:, gh * 128:(gh + 1) * 128])
    wqkv = np.ascontiguousarray(np.concatenate(blocks, axis=1))
    gw = np.ascontiguousarray(
        gwv[g * 1024:(g + 1) * 1024].reshape(8, 128).T)
    s = np.arange(128)[:, None]
    t = np.arange(128)[None, :]
    mA = ((s <= t) & (s // 64 == t // 64)).astype(md.bfloat16)
    mB = ((s < 64) & (t >= 64)).astype(md.bfloat16)
    gmask = np.ones((128, 1024), np.float32)
    gmask[:, ::64] = 0.0
    sel = np.zeros((128, 2), np.float32)
    sel[:, g] = 1.0
    return {
        "wqkv": wqkv, "wo": wo, "c_gw": gw,
        "c_mA": np.ascontiguousarray(mA), "c_mB": np.ascontiguousarray(mB),
        "c_gmask": gmask, "c_sel": sel,
        "c_identb": np.eye(128, dtype=md.bfloat16),
    }


def build_gla_program():
    nc = bass.Bass("TRN2", target_bir_lowering=False, debug=False,
                   detect_race_conditions=False)
    x_in = nc.dram_tensor("x_in", [D, TOK], F32, kind="ExternalInput").ap()
    W = {}
    for nm, shp, dt in (("wqkv", [D, 3072], F32), ("wo", [D, D], F32),
                        ("c_gw", [128, 8], F32), ("c_mA", [128, 128], BF16),
                        ("c_mB", [128, 128], BF16),
                        ("c_gmask", [128, 1024], F32),
                        ("c_sel", [128, 2], F32),
                        ("c_identb", [128, 128], BF16)):
        W[nm] = nc.dram_tensor(nm, shp, dt, kind="ExternalInput").ap()
    x_out = nc.dram_tensor("x_out", [D, TOK], F32, kind="ExternalOutput").ap()
    ag = {
        "x_my": nc.dram_tensor("ag_xmy", [D, TOK], BF16, kind="Internal").ap(),
        "x_all": nc.dram_tensor("ag_xall", [2 * D, TOK], BF16, kind="Internal").ap(),
        "a_my": nc.dram_tensor("ag_amy", [1024, SEQ], BF16, kind="Internal").ap(),
        "a_all": nc.dram_tensor("ag_aall", [2048, SEQ], BF16, kind="Internal").ap(),
    }
    od = {
        "o_raw": nc.dram_tensor("o_raw", [1024, SEQ], BF16, kind="Internal").ap(),
        "ss_my": nc.dram_tensor("ss_my", [1, SEQ], BF16, kind="Internal").ap(),
        "ss_all": nc.dram_tensor("ss_all", [2, SEQ], BF16, kind="Internal").ap(),
    }
    with contextlib.ExitStack() as ctx:
        C = Ctx(nc, ctx)
        a = gla_tiles(nc, ctx, C)
        block = ctx.enter_context(nc.Block())
        emit_consts(block, C)
        xh_marker = emit_ln(block, C, x_in)
        emit_gla(block, C, x_in, x_out, W, a, xh_marker, ag, od)
        emit_final_wait(block, C)
    return nc



def emit_ssm(block, C, x_dram, xout_dram, W, a, xh_marker, ag):
    (lx, lw, lw2, ls1, lxr, mm, mm2, mm3, ev, ev2,
     g1, g3, g4, st, st2, cc, xh, aux) = (
        C.s[n] for n in ("lx", "lw", "lw2", "ls1", "lxr", "mm", "mm2", "mm3",
                         "ev", "ev2", "g1", "g3", "g4", "st", "st2", "cc",
                         "xh", "aux"))
    b = {n: C.s[n].v for n in C.SEMS}

    flat = C.XH.ap().rearrange("p a b -> p (a b)").bitcast(F32)
    delta = flat[:, 0:4096]
    gate = flat[:, 4096:8192]
    temp = flat[:, 8192:12288]

    @block.gpsimd
    def _(e):
        e = EngWrap(e, C.chain["gpsimd"])
        e.wait_ge(C.s["g2"].h, xh_marker + 4)
        for k in range(KC):
            e.dma_start(out=ag["x_my"][k * 128:(k + 1) * 128, :],
                        in_=C.XH[:, k, :]).then_inc(g3.h, 16)
        e.wait_ge(g3.h, b["g3"] + 16 * KC)
        e.collective_compute(
            "AllGather", ALU.bypass,
            replica_groups=[[0, 1], [2, 3], [4, 5], [6, 7]],
            ins=[ag["x_my"]], outs=[ag["x_all"]],
        ).then_inc(cc.h, 1)
        for k in range(KC):
            e.dma_start(out=a["win"][:, k, :],
                        in_=W["win"][k * 128:(k + 1) * 128, :]).then_inc(lw.h, 16)
        for k in range(2):
            e.dma_start(out=a["wp"][:, k, :],
                        in_=W["wp"][k * 128:(k + 1) * 128, :]).then_inc(lw.h, 16)
        for k in range(4):
            e.dma_start(out=a["wout"][:, k, :],
                        in_=W["wout"][k * 128:(k + 1) * 128, :]).then_inc(lw.h, 16)
        e.dma_start(out=a["gbbb"][:], in_=W["c_gbbb"]).then_inc(lw.h, 16)
        e.dma_start(out=a["sel"][:], in_=W["c_sel"]).then_inc(lw.h, 16)
    g3.v += 16 * KC
    cc.v += 1
    cc_xhat = cc.v
    lw.v += 16 * 24
    lw_static = lw.v

    for th in range(2):
        b_th = {n: C.s[n].v for n in C.SEMS}

        @block.sync
        def _(e):
            e.wait_ge(cc.h, cc_xhat)
            e.wait_ge(mm.h, b_th["mm"])
            for k in range(KC):
                e.dma_start(
                    out=C.XH[:, k, :],
                    in_=ag["x_all"][th * 2048 + k * 128:
                                    th * 2048 + (k + 1) * 128, :],
                ).then_inc(ls1.h, 16)
        ls1.v += 16 * KC
        xh_ls1 = ls1.v

        @block.tensor
        def _(e):
            for n in ("ev", "ev2", "g1", "xh"):
                e.wait_ge(C.s[n].h, b_th[n])
            e.wait_ge(ls1.h, xh_ls1)
            e.wait_ge(lw.h, lw_static)
            for gidx in range(8):
                p, tt = gidx // 4, gidx % 4
                if gidx >= 2:
                    e.wait_ge(ev.h, b_th["ev"] + gidx - 1)
                pz = C.pb[gidx % 2]
                for k in range(KC):
                    ins = e.matmul(pz[:], a["win"][:, k, p * 128:(p + 1) * 128],
                                   C.XH[:, k, tt * 512:(tt + 1) * 512],
                                   start=(k == 0), stop=(k == KC - 1),
                                   skip_group_check=True)
                ins.then_inc(mm.h, 1)
        mm.v += 8

        @block.scalar
        def _(e):
            e = EngWrap(e, C.chain["scalar"])
            for gidx in range(8):
                p, tt = gidx // 4, gidx % 4
                e.wait_ge(mm.h, b_th["mm"] + gidx + 1)
                e.copy(a["z"][:, p, th * 2048 + tt * 512:
                              th * 2048 + (tt + 1) * 512],
                       C.pb[gidx % 2][:]).then_inc(ev.h, 1)
        ev.v += 8

    for u in range(4):
        dr, p = u // 2, u % 2
        bu = {n: C.s[n].v for n in C.SEMS}

        @block.tensor
        def _(e):
            e.wait_ge(ev.h, bu["ev"])
            e.wait_ge(g1.h, bu["g1"])
            for tt in range(8):
                if tt >= 2:
                    e.wait_ge(ev2.h, bu["ev2"] + 2 * (tt - 2) + 1)
                pd = C.pb[tt % 2]
                for k in range(2):
                    ins = e.matmul(pd[:],
                                   a["wp"][:, k, dr * 512 + p * 128:
                                           dr * 512 + (p + 1) * 128],
                                   a["z"][:, k, tt * 512:(tt + 1) * 512],
                                   start=(k == 0), stop=(k == 1),
                                   skip_group_check=True)
                ins.then_inc(mm.h, 1)
                if tt >= 2:
                    e.wait_ge(ev2.h, bu["ev2"] + 2 * (tt - 2) + 2)
                pg = C.pb[2 + tt % 2]
                for k in range(2):
                    ins = e.matmul(pg[:],
                                   a["wp"][:, k, dr * 512 + 256 + p * 128:
                                           dr * 512 + 256 + (p + 1) * 128],
                                   a["z"][:, k, tt * 512:(tt + 1) * 512],
                                   start=(k == 0), stop=(k == 1),
                                   skip_group_check=True)
                ins.then_inc(mm.h, 1)
        mm.v += 16

        @block.scalar
        def _(e):
            e = EngWrap(e, C.chain["scalar"])
            for tt in range(8):
                sl = slice(tt * 512, (tt + 1) * 512)
                e.wait_ge(mm.h, bu["mm"] + 2 * tt + 1)
                e.activation(delta[:, sl], C.pb[tt % 2][:], AF.Sigmoid
                             ).then_inc(ev2.h, 1)
                e.wait_ge(mm.h, bu["mm"] + 2 * tt + 2)
                if tt >= 2:
                    e.wait_ge(g1.h, bu["g1"] + tt - 1)
                e.copy(gate[:, sl], C.pb[2 + tt % 2][:]).then_inc(ev2.h, 1)
        ev2.v += 16

        @block.vector
        def _(e):
            e = EngWrap(e, C.chain["vector"])
            for tt in range(8):
                sl = slice(tt * 512, (tt + 1) * 512)
                e.wait_ge(ev2.h, bu["ev2"] + 2 * tt + 2)
                e.tensor_tensor(gate[:, sl], gate[:, sl], a["z"][:, p, sl],
                                op=ALU.mult).then_inc(g1.h, 1)
            e.wait_ge(lw.h, lw_static)
            if dr == 0:
                e.tensor_tensor_scan(temp, delta, gate, 0.0,
                                     op0=ALU.mult, op1=ALU.add)
            else:
                e.tensor_tensor_scan(temp[:, ::-1], delta[:, ::-1],
                                     gate[:, ::-1], 0.0,
                                     op0=ALU.mult, op1=ALU.add)
            e.tensor_scalar_mul(a["hs"][:, u, :], temp[:, 0:2048],
                                a["sel"][:, 0:1])
            ins = e.scalar_tensor_tensor(a["hs"][:, u, :], temp[:, 2048:4096],
                                         a["sel"][:, 1:2], a["hs"][:, u, :],
                                         op0=ALU.mult, op1=ALU.add)
            ins.then_inc(g1.h, 1)
        g1.v += 9

    bp = {n: C.s[n].v for n in C.SEMS}

    @block.sync
    def _(e):
        e.wait_ge(g1.h, bp["g1"])
        for k in range(KC):
            e.dma_start(out=C.XH[:, k, :],
                        in_=ag["x_my"][k * 128:(k + 1) * 128, :]
                        ).then_inc(ls1.h, 16)
        jj = 0
        for tt in range(4):
            for j in range(KC):
                if jj >= 2:
                    tp, jp = (jj - 2) // 16, (jj - 2) % 16
                    e.wait_ge(xh.h, bp["xh"] + 32 * tp + 16 + jp + 1)
                e.dma_start(out=a["xr"][:, jj % 2, :],
                            in_=x_dram[j * 128:(j + 1) * 128,
                                       tt * 512:(tt + 1) * 512]
                            ).then_inc(lxr.h, 16)
                e.wait_ge(xh.h, bp["xh"] + 32 * tt + 16 + j + 1)
                e.dma_start(out=xout_dram[j * 128:(j + 1) * 128,
                                          tt * 512:(tt + 1) * 512],
                            in_=a["yt"][:, jj % 2, :]
                            ).then_inc((st if jj % 2 == 0 else st2).h, 16)
                jj += 1
    ls1.v += 16 * KC
    ep_ls1 = ls1.v
    lxr.v += 16 * 64
    st.v += 16 * 32
    st2.v += 16 * 32

    for tt in range(4):
        bt = {n: C.s[n].v for n in C.SEMS}
        sl = slice(tt * 512, (tt + 1) * 512)

        @block.tensor
        def _(e):
            if tt == 0:
                e.wait_ge(ev2.h, bt["ev2"])
                e.wait_ge(g1.h, bp["g1"])
            else:
                e.wait_ge(g1.h, bp["g1"] + tt)
            for dcb in range(2):
                for m in range(8):
                    gm = dcb * 8 + m
                    if dcb == 1:
                        e.wait_ge(xh.h, bp["xh"] + 32 * tt + m + 1)
                    for j in range(4):
                        ins = e.matmul(C.pb[m][:],
                                       a["wout"][:, j, gm * 128:(gm + 1) * 128],
                                       a["hs"][:, j, sl],
                                       start=(j == 0), stop=(j == 3),
                                       skip_group_check=True)
                    ins.then_inc(mm3.h, 1)
            e.wait_ge(xh.h, bp["xh"] + 32 * tt + 10)
            for i in range(KC):
                e.wait_ge(xh.h, bp["xh"] + 32 * tt + i + 1)
                e.matmul(C.pb[0][:], C.ones_bf[:], a["ytt"][:, i, :],
                         start=(i == 0), stop=(i == KC - 1),
                         skip_group_check=True)
                e.wait_ge(ev.h, bt["ev"] + i + 1)
                ins = e.matmul(C.pb[1][:], C.ones_bf[:], C.sq[:, i % 2, :],
                               start=(i == 0), stop=(i == KC - 1),
                               skip_group_check=True)
                ins.then_inc(mm3.h, 1)
        mm3.v += 32

        @block.vector
        def _(e):
            e = EngWrap(e, C.chain["vector"])
            if tt == 0:
                e.wait_ge(ls1.h, ep_ls1)
            for gm in range(KC):
                e.wait_ge(mm3.h, bt["mm3"] + gm + 1)
                e.tensor_tensor(a["ytt"][:, gm, :], C.pb[gm % 8][:],
                                C.XH[:, gm, sl], op=ALU.add).then_inc(xh.h, 1)
            e.wait_ge(mm3.h, bt["mm3"] + 32)
            e.tensor_scalar_mul(C.mu[:, sl], C.pb[0][:], 1.0 / D)
            e.tensor_scalar_mul(C.rstd[:, sl], C.pb[1][:], 1.0 / D)
            e.tensor_tensor(C.tmp1[:], C.mu[:, sl], C.mu[:, sl], op=ALU.mult)
            e.tensor_tensor(C.rstd[:, sl], C.rstd[:, sl], C.tmp1[:],
                            op=ALU.subtract)
            e.tensor_scalar_add(C.rstd[:, sl], C.rstd[:, sl], LN_EPS)
            e.reciprocal(C.rstd[:, sl], C.rstd[:, sl]).then_inc(g1.h, 1)
            for j in range(KC):
                jj = tt * KC + j
                e.wait_ge(ev2.h, bt["ev2"] + 1)
                e.wait_ge(lxr.h, bp["lxr"] + 16 * (jj + 1))
                if jj >= 2:
                    psem = st if jj % 2 == 0 else st2
                    pbv = bp["st"] if jj % 2 == 0 else bp["st2"]
                    e.wait_ge(psem.h, pbv + 16 * (jj // 2))
                e.tensor_tensor(a["yt"][:, jj % 2, :], a["ytt"][:, j, :],
                                C.mu[:, sl], op=ALU.subtract)
                e.tensor_tensor(a["yt"][:, jj % 2, :], a["yt"][:, jj % 2, :],
                                C.rstd[:, sl], op=ALU.mult)
                e.tensor_scalar_mul(a["yt"][:, jj % 2, :], a["yt"][:, jj % 2, :],
                                    a["gbbb"][:, j:j + 1])
                e.tensor_scalar_add(a["yt"][:, jj % 2, :], a["yt"][:, jj % 2, :],
                                    a["gbbb"][:, KC + j:KC + j + 1])
                e.tensor_tensor(a["yt"][:, jj % 2, :], a["yt"][:, jj % 2, :],
                                a["xr"][:, jj % 2, :], op=ALU.add
                                ).then_inc(xh.h, 1)
        xh.v += 32
        g1.v += 1

        @block.scalar
        def _(e):
            e = EngWrap(e, C.chain["scalar"])
            for i in range(KC):
                e.wait_ge(xh.h, bp["xh"] + 32 * tt + i + 1)
                if i >= 2:
                    e.wait_ge(mm3.h, bt["mm3"] + 16 + i - 1)
                e.activation(C.sq[:, i % 2, :], a["ytt"][:, i, :], AF.Square
                             ).then_inc(ev.h, 1)
            e.wait_ge(g1.h, bp["g1"] + tt + 1)
            e.activation(C.rstd[:, sl], C.rstd[:, sl], AF.Sqrt).then_inc(ev2.h, 1)
        ev.v += 16
        ev2.v += 1


def ssm_tiles(nc, ctx, C):
    a = {}
    def sb(name, shape, dt):
        a[name] = ctx.enter_context(nc.sbuf_tensor("m_" + name, shape, dt)).ap()
    sb("z", [128, 2, 4096], BF16)
    sb("hs", [128, 4, 2048], BF16)
    sb("win", [128, KC, 256], BF16)
    sb("wp", [128, 2, 1024], BF16)
    sb("wout", [128, 4, 2048], BF16)
    sb("ytt", [128, KC, 512], BF16)
    sb("gbbb", [128, 32], F32)
    sb("sel", [128, 2], F32)
    sb("yt", [128, 2, 512], F32)
    sb("xr", [128, 2, 512], F32)
    return a


def ssm_host_inputs(inp, core):
    g = core % 2
    win = np.asarray(inp["L0_attn_a_W_in"], np.float32)
    wf = np.asarray(inp["L0_attn_a_W_fwd"], np.float32)
    wb = np.asarray(inp["L0_attn_a_W_bwd"], np.float32)
    wout = np.asarray(inp["L0_attn_a_W_out"], np.float32)
    gam = np.asarray(inp["L0_attn_a_gamma"], np.float32).reshape(-1)
    bet = np.asarray(inp["L0_attn_a_beta"], np.float32).reshape(-1)
    wp = np.ascontiguousarray(np.concatenate([wf, wb], axis=1))
    gbbb = np.concatenate([gam.reshape(16, 128).T, bet.reshape(16, 128).T],
                          axis=1)
    sel = np.zeros((128, 2), np.float32)
    sel[:, g] = 1.0
    return {"win": win, "wp": wp, "wout": wout,
            "c_gbbb": np.ascontiguousarray(gbbb), "c_sel": sel}


def build_ssm_program():
    nc = bass.Bass("TRN2", target_bir_lowering=False, debug=False,
                   detect_race_conditions=False)
    x_in = nc.dram_tensor("x_in", [D, TOK], F32, kind="ExternalInput").ap()
    W = {}
    for nm, shp, dt in (("win", [D, 256], F32), ("wp", [256, 1024], F32),
                        ("wout", [512, D], F32), ("c_gbbb", [128, 32], F32),
                        ("c_sel", [128, 2], F32)):
        W[nm] = nc.dram_tensor(nm, shp, dt, kind="ExternalInput").ap()
    x_out = nc.dram_tensor("x_out", [D, TOK], F32, kind="ExternalOutput").ap()
    ag = {
        "x_my": nc.dram_tensor("ag_xmy", [D, TOK], BF16, kind="Internal").ap(),
        "x_all": nc.dram_tensor("ag_xall", [2 * D, TOK], BF16, kind="Internal").ap(),
    }
    with contextlib.ExitStack() as ctx:
        C = Ctx(nc, ctx)
        a = ssm_tiles(nc, ctx, C)
        block = ctx.enter_context(nc.Block())
        emit_consts(block, C)
        xh_marker = emit_ln(block, C, x_in)
        emit_ssm(block, C, x_in, x_out, W, a, xh_marker, ag)
        emit_final_wait(block, C)
    return nc



def _fused_barrier(block, C):
    vals = [(s.h, s.v) for s in C.s.values()] + \
           [(s.h, s.v) for s in C.chain.values()]
    for eng in ("sync", "scalar", "vector", "gpsimd", "tensor"):
        @getattr(block, eng)
        def _(e, _vals=tuple(vals)):
            for h, v in _vals:
                if v:
                    e.wait_ge(h, v)


def build_fused_program():
    nc = bass.Bass("TRN2", target_bir_lowering=False, debug=False,
                   detect_race_conditions=False)
    xs = [None] * 9
    xs[0] = nc.dram_tensor("x_in", [D, TOK], F32, kind="ExternalInput").ap()
    for k in range(1, 8):
        xs[k] = nc.dram_tensor("x_st%d" % k, [D, TOK], F32, kind="Internal").ap()
    xs[8] = nc.dram_tensor("x_out", [D, TOK], F32, kind="ExternalOutput").ap()

    def ein(name, shape, dt=F32):
        return nc.dram_tensor(name, shape, dt, kind="ExternalInput").ap()

    c_sel = ein("c_sel", [128, 2], F32)
    c_ident = ein("c_ident", [128, 128], BF16)
    c_mask4 = ein("c_mask4", [512, 512], BF16)
    c_esel = ein("c_esel", [8, 1024], BF16)

    Wssm = {"win": ein("s_win", [D, 256]), "wp": ein("s_wp", [256, 1024]),
            "wout": ein("s_wout", [512, D]), "c_gbbb": ein("s_gbbb", [128, 32]),
            "c_sel": c_sel}
    router = ein("router", [D, 8])
    gate = ein("gate", [8, D, 1024])
    up = ein("up", [8, D, 1024])
    down = ein("down", [8, 1024, D])
    Wd = {"wq": ein("d_wq", [D, 1024]), "wkv": ein("d_wkv", [D, 2048]),
          "wo": ein("d_wo", [D, D]), "c_mask4": c_mask4, "c_sel": c_sel,
          "c_ident": c_ident, "lq1": ein("lq1", [128, 1]),
          "lk1": ein("lk1", [128, 1]), "lq2": ein("lq2", [128, 1]),
          "lk2": ein("lk2", [128, 1]), "subln": ein("subln", [128, 2])}
    Wg = {"wq": ein("g_wq", [D, 1024]), "wkv": ein("g_wkv", [D, 512]),
          "wo": ein("g_wo", [D, D]), "cosq": ein("cosq", [128, SEQ]),
          "sinq": ein("sinq", [128, SEQ]), "costm": ein("costm", [SEQ, 128]),
          "sintm": ein("sintm", [SEQ, 128]),
          "c_rot": ein("c_rot", [128, 128], BF16),
          "c_mask4": c_mask4, "c_sel": c_sel, "c_ident": c_ident}
    w_fc = ein("w_fc", [8192, D])
    w_proj = ein("w_proj", [D, 8192])
    w1t = nc.dram_tensor("w1t", [D, 8192], BF16, kind="Internal").ap()
    w2t = nc.dram_tensor("w2t", [8192, D], BF16, kind="Internal").ap()
    Wu = {"wqkv": ein("ub_wqkv", [D, 3072]), "wo": ein("ub_wo", [D, D]),
          "c_gw": ein("c_gw", [128, 8]), "c_mA": ein("c_mA", [128, 128], BF16),
          "c_mB": ein("c_mB", [128, 128], BF16),
          "c_gmask": ein("c_gmask", [128, 1024]), "c_sel": c_sel,
          "c_identb": c_ident}

    def mk_ag(pfx, with_a=True):
        d = {"x_my": nc.dram_tensor(pfx + "_xmy", [D, TOK], BF16,
                                    kind="Internal").ap(),
             "x_all": nc.dram_tensor(pfx + "_xall", [2 * D, TOK], BF16,
                                     kind="Internal").ap()}
        if with_a:
            d["a_my"] = nc.dram_tensor(pfx + "_amy", [1024, SEQ], BF16,
                                       kind="Internal").ap()
            d["a_all"] = nc.dram_tensor(pfx + "_aall", [2048, SEQ], BF16,
                                        kind="Internal").ap()
        return d

    ag_ssm = mk_ag("agS", False)
    ag_d = mk_ag("agD")
    ag_g = mk_ag("agG")
    ag_u = mk_ag("agU")
    od = {"o_raw": nc.dram_tensor("o_raw", [1024, SEQ], BF16, kind="Internal").ap(),
          "ss_my": nc.dram_tensor("ss_my", [1, SEQ], BF16, kind="Internal").ap(),
          "ss_all": nc.dram_tensor("ss_all", [2, SEQ], BF16, kind="Internal").ap()}

    with contextlib.ExitStack() as ctx:
        C = Ctx(nc, ctx)
        C.dbg = None
        block = ctx.enter_context(nc.Block())
        emit_consts(block, C)

        with contextlib.ExitStack() as s:
            a = ssm_tiles(nc, s, C)
            xh = emit_ln(block, C, xs[0])
            emit_ssm(block, C, xs[0], xs[1], Wssm, a, xh, ag_ssm)
        _fused_barrier(block, C)

        with contextlib.ExitStack() as s:
            m = moe_tiles(nc, s, pfx="m1_")
            C.c_ident = s.enter_context(nc.sbuf_tensor("m1_c_ident_sb", [128, 128], BF16))
            C.c_esel = s.enter_context(nc.sbuf_tensor("m1_c_esel_sb", [8, 1024], BF16))
            emit_load_consts(block, C, {"c_ident": c_ident, "c_esel": c_esel})
            C.aux_need = C.s["aux"].v
            m["c_ident"] = C.c_ident.ap()
            m["c_esel"] = C.c_esel.ap()
            xh = emit_ln(block, C, xs[1])
            emit_moe(block, C, xs[1], xs[2], router, gate, up, down, m, xh)
        _fused_barrier(block, C)

        with contextlib.ExitStack() as s:
            a = dattn_tiles(nc, s, C)
            xh = emit_ln(block, C, xs[2])
            emit_dattn(block, C, xs[2], xs[3], Wd, a, xh, ag_d)
        _fused_barrier(block, C)

        with contextlib.ExitStack() as s:
            m = moe_tiles(nc, s, pfx="m2_")
            C.c_ident = s.enter_context(nc.sbuf_tensor("m2_c_ident_sb", [128, 128], BF16))
            C.c_esel = s.enter_context(nc.sbuf_tensor("m2_c_esel_sb", [8, 1024], BF16))
            emit_load_consts(block, C, {"c_ident": c_ident, "c_esel": c_esel})
            C.aux_need = C.s["aux"].v
            m["c_ident"] = C.c_ident.ap()
            m["c_esel"] = C.c_esel.ap()
            xh = emit_ln(block, C, xs[3])
            emit_moe(block, C, xs[3], xs[4], router, gate, up, down, m, xh)
        _fused_barrier(block, C)

        with contextlib.ExitStack() as s:
            a = gqa_tiles(nc, s, C)
            xh = emit_ln(block, C, xs[4])
            emit_gqa(block, C, xs[4], xs[5], Wg, a, xh, ag_g)
        _fused_barrier(block, C)

        with contextlib.ExitStack() as s:
            arena = s.enter_context(nc.sbuf_tensor("arena6", [128, 32768], BF16))
            ar = arena.ap()
            tern = {
                "wb": ar[:, 0:8192].rearrange("p (b c) -> p b c", b=2),
                "ab": ar[:, 8192:12288],
                "tq": ar[:, 12288:20480].rearrange("p (b c) -> p b c", b=2),
            }
            sc = s.enter_context(nc.sbuf_tensor("sc6", [128, 64], F32))
            scb = s.enter_context(nc.sbuf_tensor("scb6", [128, 64], BF16))
            rec = s.enter_context(nc.sbuf_tensor("rec6", [128, 64], F32))
            tbt = ar[:, 24576:28672].rearrange("p (r m) -> p r m", m=128)
            idt = ar[:, 28672:28800]
            tern.update(sc=sc.ap(), scb=scb.ap(), rec=rec.ap(), tb=tbt)
            f = {
                "H": ar[:, 0:32768].rearrange("p (h t) -> p h t", h=64),
                "w1s": s.enter_context(
                    nc.sbuf_tensor("w1s6", [128, 2, KC, HB_W1 * 128], BF16)),
                "w2s": s.enter_context(nc.sbuf_tensor("w2s6", [128, 2, 1024], BF16)),
                "rl": s.enter_context(nc.sbuf_tensor("rl6", [128, 2, 512], BF16)),
                "h2": s.enter_context(nc.sbuf_tensor("h26", [128, 2, 512], BF16)),
                "rt": s.enter_context(nc.sbuf_tensor("rt6", [128, 512], F32)),
                "yt": s.enter_context(nc.sbuf_tensor("yt6", [128, 8, 512], F32)),
                "xr": s.enter_context(nc.sbuf_tensor("xr6", [128, 8, 512], F32)),
            }
            xh = emit_ln(block, C, xs[5])

            @block.gpsimd
            def _(e):
                e.dma_start(out=idt, in_=c_ident).then_inc(C.s["li"].h, 16)
            b_li = C.s["li"].v
            C.s["li"].v += 16
            irdy = (C.s["li"].h, b_li + 16)
            emit_ternarize(block, C, w_fc, w1t, 8192, D, tern, colsplit=1,
                           ident=idt, ident_ready=irdy)
            emit_ternarize(block, C, w_proj, w2t, D, 8192, tern, colsplit=2,
                           ident=idt, ident_ready=irdy)
            tern_done = C.s["g3"].v
            emit_l1_ffn_body(block, C, xs[5], xs[6], w1t, w2t, f, xh, tern_done)
        _fused_barrier(block, C)

        with contextlib.ExitStack() as s:
            a = gla_tiles(nc, s, C)
            xh = emit_ln(block, C, xs[6])
            emit_gla(block, C, xs[6], xs[7], Wu, a, xh, ag_u, od)
        _fused_barrier(block, C)

        with contextlib.ExitStack() as s:
            arena = s.enter_context(nc.sbuf_tensor("arena8", [128, 32768], BF16))
            ar = arena.ap()
            f = {
                "H": ar[:, 0:32768].rearrange("p (h t) -> p h t", h=64),
                "w1s": s.enter_context(
                    nc.sbuf_tensor("w1s8", [128, 2, KC, HB_W1 * 128], BF16)),
                "w2s": s.enter_context(nc.sbuf_tensor("w2s8", [128, 2, 1024], BF16)),
                "rl": s.enter_context(nc.sbuf_tensor("rl8", [128, 2, 512], BF16)),
                "h2": s.enter_context(nc.sbuf_tensor("h28", [128, 2, 512], BF16)),
                "rt": s.enter_context(nc.sbuf_tensor("rt8", [128, 512], F32)),
                "yt": s.enter_context(nc.sbuf_tensor("yt8", [128, 8, 512], F32)),
                "xr": s.enter_context(nc.sbuf_tensor("xr8", [128, 8, 512], F32)),
            }
            xh = emit_ln(block, C, xs[7])
            tern_done = C.s["g3"].v
            emit_l1_ffn_body(block, C, xs[7], xs[8], w1t, w2t, f, xh, tern_done)
        _fused_barrier(block, C)
    return nc


_FUSED_CACHE = []


def _fused_program():
    if not _FUSED_CACHE:
        _FUSED_CACHE.append(build_fused_program())
    return _FUSED_CACHE[0]


def kernel(**inputs):
    x = np.asarray(inputs["L0_attn_a_x"], np.float32)
    nc = _fused_program()
    consts = host_consts()
    base = {
        "router": np.asarray(inputs["L0_ffn_router_w"], np.float32),
        "gate": np.asarray(inputs["L0_ffn_gate_proj"], np.float32),
        "up": np.asarray(inputs["L0_ffn_up_proj"], np.float32),
        "down": np.asarray(inputs["L0_ffn_down_proj"], np.float32),
        "c_esel": consts["c_esel"],
        "w_fc": np.asarray(inputs["L1_ffn_W_fc"], np.float32),
        "w_proj": np.asarray(inputs["L1_ffn_W_proj"], np.float32),
    }
    sh = shard_x(x)
    maps = []
    for c in range(NCORE):
        sm = ssm_host_inputs(inputs, c)
        dm = dattn_host_inputs(inputs, c)
        gm = gqa_host_inputs(inputs, c)
        um = gla_host_inputs(inputs, c)
        mm = dict(base)
        mm["x_in"] = sh[c]
        mm["c_sel"] = sm["c_sel"]
        mm["c_ident"] = dm["c_ident"]
        mm["c_mask4"] = dm["c_mask4"]
        mm["s_win"] = sm["win"]
        mm["s_wp"] = sm["wp"]
        mm["s_wout"] = sm["wout"]
        mm["s_gbbb"] = sm["c_gbbb"]
        mm["d_wq"] = dm["wq"]
        mm["d_wkv"] = dm["wkv"]
        mm["d_wo"] = dm["wo"]
        for k in ("lq1", "lk1", "lq2", "lk2", "subln"):
            mm[k] = dm[k]
        mm["g_wq"] = gm["wq"]
        mm["g_wkv"] = gm["wkv"]
        mm["g_wo"] = gm["wo"]
        for k in ("cosq", "sinq", "costm", "sintm", "c_rot"):
            mm[k] = gm[k]
        mm["ub_wqkv"] = um["wqkv"]
        mm["ub_wo"] = um["wo"]
        for k in ("c_gw", "c_mA", "c_mB", "c_gmask"):
            mm[k] = um[k]
        maps.append(mm)
    res = run_bass_kernel_spmd(nc, maps, core_ids=list(range(NCORE)))
    return unshard_x([res.results[c]["x_out"] for c in range(NCORE)])
